# version 15
# baseline (speedup 1.0000x reference)
import numpy as np
import concourse.bass as bass
import concourse.mybir as mybir
from concourse.bass_utils import run_bass_kernel_spmd

F32 = mybir.dt.float32
BF16 = mybir.dt.bfloat16
FP8 = mybir.dt.float8e4
AF = mybir.ActivationFunctionType
ALU = mybir.AluOpType
AX = mybir.AxisListType.X

C23 = 12582912.0
C52 = 6755399441055744.0
EPS = 1e-5
NCH, D, INNER = 16, 1024, 4096
SEQ = NCH * 256
STAGE = 3
DEBUG_TAPS = ()
LAST_COL_IDS = {}


class Res:
    __slots__ = ("name", "writer", "readers")

    def __init__(self, name):
        self.name = name
        self.writer = None
        self.readers = []


class EngS:
    def __init__(self, name, body, step=1):
        self.name, self.body, self.step = name, body, step
        self.n = 0
        self.sem = None
        self.pending = []
        self.last_op = None
        self.max_waited = {}


class Prog:
    def __init__(self):
        self.ops = {k: [] for k in ("tensor", "vector", "scalar", "gpsimd", "sync")}

    def fence(self, e):
        if e.last_op is None:
            assert not e.pending
            return
        e.n += e.step
        e.last_op["inc"] = (e, e.step)
        for kind, r in e.pending:
            if kind == "w":
                r.writer = (e, e.n)
                r.readers = []
            else:
                r.readers.append((e, e.n))
        e.pending = []
        e.last_op = None

    def op(self, e, fn, reads=(), writes=(), tag=""):
        waits = {}

        def need(src):
            if src is None:
                return
            s, v = src
            if s is e or e.max_waited.get(s, -1) >= v:
                return
            waits[s] = max(waits.get(s, 0), v)

        for r in reads:
            need(r.writer)
        for r in writes:
            need(r.writer)
            for rd in r.readers:
                need(rd)
        for s, v in waits.items():
            if v > s.n:
                raise RuntimeError(f"wait on unfenced {s.name}={v} (have {s.n}) at {tag}")
            e.max_waited[s] = v
        rec = {"fn": fn, "waits": list(waits.items()), "inc": None, "tag": tag}
        self.ops[e.body].append(rec)
        e.last_op = rec
        for r in reads:
            e.pending.append(("r", r))
        for r in writes:
            e.pending.append(("w", r))
        return rec

    def handoff(self, dsts, srcs):
        ev = []
        for r in srcs:
            if r.writer is not None:
                ev.append(r.writer)
            ev.extend(r.readers)
        for d in dsts:
            d.writer = None
            d.readers = list(ev)

    def run_body(self, engobj, body):
        for rec in self.ops[body]:
            for s, v in rec["waits"]:
                engobj.wait_ge(s.sem, v)
            instr = rec["fn"](engobj)
            if rec["inc"] is not None:
                s, k = rec["inc"]
                instr.then_inc(s.sem, k)


def build(stage=STAGE):
    nc = bass.Bass("TRN2", target_bir_lowering=False, debug=False)
    P = Prog()
    pe = EngS("pe", "tensor")
    dv = EngS("dv", "vector")
    ac = EngS("ac", "scalar")
    gp = EngS("gp", "gpsimd")
    d_x = EngS("d_x", "sync", 16)
    d_w = EngS("d_w", "gpsimd", 16)
    d_o = EngS("d_o", "sync", 16)
    d_t = EngS("d_t", "sync", 16)
    engs = [pe, dv, ac, gp, d_x, d_w, d_o, d_t]

    x_h = nc.dram_tensor("x", [SEQ, D], F32, kind="ExternalInput").ap()
    wq_h = nc.dram_tensor("wq", [D, 512], F32, kind="ExternalInput").ap()
    wk_h = nc.dram_tensor("wk", [D, 512], F32, kind="ExternalInput").ap()
    wv_h = nc.dram_tensor("wv", [D, 512], F32, kind="ExternalInput").ap()
    wo_h = nc.dram_tensor("wo", [512, D], F32, kind="ExternalInput").ap()
    bet_h = nc.dram_tensor("betas", [1, 512], F32, kind="ExternalInput").ap()
    wg_h = nc.dram_tensor("wg", [INNER, D], F32, kind="ExternalInput").ap()
    wu_h = nc.dram_tensor("wu", [INNER, D], F32, kind="ExternalInput").ap()
    wd_h = nc.dram_tensor("wd", [D, INNER], F32, kind="ExternalInput").ap()
    y_h = nc.dram_tensor("y", [SEQ, D], F32, kind="ExternalOutput").ap()

    ctx = []

    def sb(name, shape, dt):
        g = nc.sbuf_tensor(name, shape, dt)
        t = g.__enter__()
        ctx.append(g)
        return t

    def pst(name, shape, dt):
        g = nc.psum_tensor(name, shape, dt)
        t = g.__enter__()
        ctx.append(g)
        return t

    wgT = sb("wgT", [128, 8 * INNER], FP8)
    wuT = sb("wuT", [128, 8 * INNER], FP8)
    wdT = sb("wdT", [128, 32 * 1024], FP8)
    wqb = sb("wqb", [128, 8 * 512], BF16)
    wkb = sb("wkb", [128, 8 * 512], BF16)
    wvb = sb("wvb", [128, 8 * 512], BF16)
    wob = sb("wob", [128, 4 * 1024], BF16)
    XIN = sb("XIN", [128, 2048], F32)
    X1 = sb("X1", [128, 2048], F32)
    X2 = sb("X2", [128, 2048], F32)
    HT = sb("HT", [128, 4096], F32)
    HQ = sb("HQ", [128, 4096], BF16)
    FMP = sb("FMP", [128, 4096], BF16)
    XQ = sb("XQ", [128, 1024], BF16)
    XQF = sb("XQF", [128, 1024], BF16)
    SCR = sb("SCR", [128, 2048], F32)
    SC2 = sb("SC2", [128, 1024], F32)
    HTb = HT.ap().bitcast(BF16)

    QB = HTb[:, 0:1024]
    KB = HTb[:, 1024:2048]
    SQF = HTb[:, 2048:3072]
    SKF = HTb[:, 3072:4096]
    SKT = HTb[:, 4096:5120]
    VT = HTb[:, 5120:6144]
    VP = HTb[:, 6144:7168]
    ATT = HTb[:, 7168:8192]
    LNB = HQ.ap()[:, 0:2048]
    EB0 = HQ.ap()[:, 2048:3072]
    EB1 = HQ.ap()[:, 0:1024]
    ETB = HQ.ap()[:, 3072:3584]
    ETB1 = HQ.ap()[:, 3584:4096]
    ATF = FMP.ap()[:, 0:1024]
    LNF = FMP.ap()[:, 1024:3072]
    MEMZ = sb("MEMZ", [128, 260], BF16)
    ZS = sb("ZS", [128, 4], F32)
    GB = sb("GB", [128, 256], F32)
    OGF = sb("OGF", [128, 512], BF16)
    COLS = sb("COLS", [128, 176], F32)
    IDN = sb("IDN", [128, 128], BF16)
    ONE1 = sb("ONE1", [1, 128], BF16)
    ONE1F = sb("ONE1F", [1, 128], F32)
    ONEC = sb("ONEC", [128, 1], F32)
    ROWS = SCR.ap()[0:1, 0:1024]
    ROWB = HTb[0:1, 6144:7168]
    STAT = sb("STAT", [128, 16], F32)
    WST = HT.ap()[:, 0:4096]
    WQ3 = HQ.ap()[:, 0:2048]

    pT0 = pst("pT0", [128, 1024], BF16)
    pT1 = pst("pT1", [128, 1024], BF16)
    pW0 = pst("pW0", [128, 512], F32)
    pW1 = pst("pW1", [128, 512], F32)
    pW2 = pst("pW2", [128, 512], F32)
    pD = pst("pD", [128, 1024], F32)
    pM = pst("pM", [128, 512], F32)

    R = Res
    r_wgT, r_wuT, r_wdT = R("wgT"), R("wuT"), R("wdT")
    r_awt = R("attn_w")
    r_xin, r_x1, r_x2 = R("XIN"), R("X1"), R("X2")
    r_ht, r_xq, r_xqf = R("HT"), R("XQ"), R("XQF")
    r_hqq = [R(f"HQ{i}") for i in range(4)]
    r_fmpq = [R(f"FMP{i}") for i in range(4)]
    r_scr, r_sc2 = R("SCR"), R("SC2")
    r_u, r_t1 = R("U"), R("T1")
    r_s2b0, r_s2b1, r_s2b2 = R("s2b0"), R("s2b1"), R("s2b2")
    r_wstA, r_wstB = R("wstA"), R("wstB")
    r_lnb, r_lnf = R("LNB"), R("LNF")
    r_qb, r_kb, r_sqf, r_skf, r_skt = R("QB"), R("KB"), R("SQF"), R("SKF"), R("SKT")
    r_vt, r_vp, r_eb, r_etb = R("VT"), R("VP"), R("EB"), R("ETB")
    r_eb1 = R("EB1")
    r_att, r_atf = R("ATT"), R("ATF")
    r_memz, r_zs, r_gb, r_ogf = R("MEMZ"), R("ZS"), R("GB"), R("OGF")
    r_cols, r_idn, r_ones = R("COLS"), R("IDN"), R("ONES")
    r_rows, r_stat = R("ROWS"), R("STAT")
    r_wst, r_wq3 = R("WST"), R("WQ3")
    r_tb1, r_tb3 = R("TQ1"), R("TQ3")
    r_tcol = R("TCOL")
    r_pT0, r_pT1 = R("pT0"), R("pT1")
    r_pW0, r_pW1, r_pW2, r_pD, r_pM = R("pW0"), R("pW1"), R("pW2"), R("pD"), R("pM")

    r_qb2, r_kb2 = r_qb, r_kb
    FFN_SIDE = [r_ht, r_u, r_t1, r_s2b0, r_s2b1, r_s2b2] + r_hqq + r_fmpq
    ATTN_SIDE = [r_qb, r_kb, r_sqf, r_skf, r_skt, r_vt, r_vp, r_att,
                 r_lnb, r_eb, r_eb1, r_etb, r_atf, r_lnf, r_sc2]

    col_ids = {}
    global LAST_COL_IDS
    LAST_COL_IDS = col_ids

    def col(name):
        if name not in col_ids:
            col_ids[name] = len(col_ids)
            assert len(col_ids) <= 176, "out of COLS"
        i = col_ids[name]
        return COLS[:, i:i + 1]

    def dbg(name, ap, reads):
        if name not in DEBUG_TAPS:
            return
        h = nc.dram_tensor("dbg_" + name, list(ap.shape), ap.dtype,
                           kind="ExternalOutput").ap()
        P.op(d_o, lambda s, h=h, ap=ap: s.dma_start(out=h, in_=ap),
             reads=tuple(reads), tag="dbg_" + name)
        P.fence(d_o)

    def g_setup(g):
        g.memset(IDN[:, :], 0.0)
        g.affine_select(out=IDN[:, :], in_=IDN[:, :], compare_op=ALU.not_equal,
                        fill=1.0, base=0, pattern=[[-1, 128]], channel_multiplier=1)
        g.memset(ONE1[:, :], 1.0)
        g.memset(ONE1F[:, :], 1.0)
        g.memset(ONEC[:, :], 1.0)
        g.memset(ZS[:, :], 1.0 / 64.0)
        return g.memset(MEMZ[:, :], 0.0)
    P.op(gp, g_setup, writes=(r_idn, r_ones, r_zs, r_memz), tag="setup")
    P.fence(gp)

    def d_flag(v):
        v.memset(col("k75"), 0.75)
        v.drain()
        v.tensor_scalar(col("flag"), col("k75"), C52, None, op0=ALU.add)
        v.drain()
        v.tensor_scalar(col("flag"), col("flag"), C52, None, op0=ALU.subtract)
        v.drain()
        return v.tensor_scalar(col("om"), col("flag"), -1.0, 1.0, op0=ALU.mult, op1=ALU.add)
    P.op(dv, d_flag, writes=(r_cols,), tag="flag")
    P.fence(dv)
    dbg("cols_flag", COLS[:, :], [r_cols])

    for (h, t, kk) in ((wq_h, wqb, 8), (wk_h, wkb, 8), (wv_h, wvb, 8), (wo_h, wob, 4)):
        def ld(g, h=h, t=t, kk=kk):
            return g.dma_start(out=t.ap().rearrange("p (k n) -> p k n", k=kk),
                               in_=h.rearrange("(k p) n -> p k n", p=128))
        P.op(d_w, ld, writes=(r_awt,), tag="wload")
        P.fence(d_w)
    P.op(d_w, lambda g: g.dma_start(out=ROWS[0:1, 0:512], in_=bet_h),
         writes=(r_rows, r_sc2), tag="betas")
    P.fence(d_w)

    def x_load(c):
        def ld(s, c=c):
            return s.dma_start(
                out=XIN.ap().rearrange("p (t n) -> p t n", t=2),
                in_=x_h[c * 256:(c + 1) * 256, :].rearrange("(t p) n -> p t n", p=128))
        P.op(d_x, ld, writes=(r_xin,), tag=f"xload{c}")
        P.fence(d_x)

    x_load(0)

    P.op(ac, lambda a: a.activation(ROWS[0:1, 512:1024], ROWS[0:1, 0:512], AF.Sigmoid),
         reads=(r_rows,), writes=(r_rows,), tag="sig")
    P.fence(ac)

    def d_ogrow(v):
        v.tensor_scalar(ROWS[0:1, 0:512], ROWS[0:1, 512:1024], -1.0, 1.0,
                        op0=ALU.mult, op1=ALU.add)
        v.tensor_copy(ROWB[0:1, 0:512], ROWS[0:1, 0:512])
        return v.tensor_copy(ROWB[0:1, 512:1024], ROWS[0:1, 512:1024])
    P.op(dv, d_ogrow, reads=(r_rows, r_sc2), writes=(r_rows, r_ht), tag="ogrow")
    P.fence(dv)
    dbg("rowb", ROWB[0:1, 0:1024], [r_rows])
    dbg("rows", ROWS[0:1, 0:1024], [r_rows])

    P.op(pe, lambda t: t.matmul(pW0[:, 0:512], ONE1[0:1, :], ROWB[0:1, 0:512],
                                start=True, stop=True),
         reads=(r_rows, r_ones), writes=(r_pW0,), tag="ogbc")
    P.fence(pe)
    P.op(dv, lambda v: v.tensor_copy(OGF[:, :], pW0[:, 0:512]),
         reads=(r_pW0,), writes=(r_ogf,), tag="ogf")
    P.fence(dv)
    dbg("ogf", OGF[:, :], [r_ogf])
    dbg("gbf", GB[:, :], [r_gb])
    for m in range(4):
        for par in range(2):
            def gbc(t, m=m, par=par):
                return t.matmul(pW1[par * 64:(par + 1) * 64, m * 64:(m + 1) * 64],
                                ONE1[0:1, 0:64],
                                ROWB[0:1, 512 + (2 * m + par) * 64:512 + (2 * m + par + 1) * 64],
                                start=True, stop=True,
                                tile_position=(0, par * 64))
            P.op(pe, gbc, reads=(r_rows, r_ones), writes=(r_pW1,), tag="gbc")
    P.fence(pe)
    P.op(dv, lambda v: v.tensor_copy(GB[:, :], pW1[:, 0:256]),
         reads=(r_pW1,), writes=(r_gb,), tag="gbf")
    P.fence(dv)

    def tern_A_tile(wname, src_fn, i):
        acc, tmp = col("acc_" + wname), col("tmp_" + wname)
        off = (i % 2) * 1024
        def ld(sq, i=i, off=off):
            return sq.dma_start(out=WST[:, off:off + 1024], in_=src_fn(i))
        P.op(d_x, ld, writes=(r_wstA,), tag="tldA")
        P.fence(d_x)
        def a_abs(a, off=off):
            return a.activation(SCR[:, 1024:2048], WST[:, off:off + 1024],
                                AF.Abs, accum_out=tmp)
        P.op(ac, a_abs, reads=(r_wstA,), writes=(r_t1, r_tcol), tag="abs")
        P.fence(ac)
        if i == 0:
            P.op(dv, lambda v: v.tensor_copy(acc, tmp),
                 reads=(r_tcol,), writes=(r_tcol,), tag="acc")
        else:
            def d_acc(v):
                v.drain()
                return v.tensor_add(acc, acc, tmp)
            P.op(dv, d_acc, reads=(r_tcol,), writes=(r_tcol,), tag="acc")
        P.fence(dv)

    def tern_scale(wname):
        acc = col("acc_" + wname)
        P.op(pe, lambda t: t.matmul(pW2[0:1, 0:1], acc, ONEC[:, :],
                                    start=True, stop=True),
             reads=(r_tcol, r_ones), writes=(r_pW2,), tag="tsum")
        P.fence(pe)
        def d_scale(v):
            v.tensor_scalar(STAT[0:1, 14:15], pW2[0:1, 0:1],
                            1.0 / (INNER * D), 1e-5, op0=ALU.mult, op1=ALU.max)
            v.drain()
            return v.reciprocal(STAT[0:1, 15:16], STAT[0:1, 14:15])
        P.op(dv, d_scale, reads=(r_pW2,), writes=(r_stat,), tag="scale")
        P.fence(dv)
        P.op(pe, lambda t: t.matmul(pW2[:, 4:6], ONE1F[0:1, :],
                                    STAT[0:1, 14:16], start=True, stop=True),
             reads=(r_stat, r_ones), writes=(r_pW2,), tag="scbc")
        P.fence(pe)
        def d_cols(v, wname=wname):
            v.tensor_copy(col("swrec_" + wname), pW2[:, 4:5])
            v.tensor_copy(col("wsc_" + wname), pW2[:, 5:6])
            v.drain()
            return v.tensor_mul(col("wsck_" + wname), col("wsc_" + wname),
                                col("flag"))
        P.op(dv, d_cols, reads=(r_pW2, r_cols), writes=(r_cols,), tag="wcols")
        P.fence(dv)

    def tern_B_parity(i):
        if i % 2 == 0:
            q1, q3 = SC2[:, 0:1024], SCR[:, 0:1024]
            rq1, rq3 = (r_s2b0, r_s2b1, r_sc2), (r_u,)
        else:
            q1, q3 = X1[:, 0:1024], X1[:, 1024:2048]
            rq1, rq3 = (r_tb1,), (r_tb3,)
        pb, rpb = (pT0, r_pT0) if i % 2 == 0 else (pT1, r_pT1)
        return q1, q3, rq1, rq3, pb, rpb

    def tern_B_front(wname, src_fn, i):
        wsc, wsck = col("wsc_" + wname), col("wsck_" + wname)
        off = (2 + i % 2) * 1024
        q1, q3, rq1, rq3, pb, rpb = tern_B_parity(i)
        def ld(g, i=i, off=off):
            return g.dma_start(out=WST[:, off:off + 1024], in_=src_fn(i))
        P.op(d_w, ld, writes=(r_wstB,), tag="tldB")
        P.fence(d_w)
        def a_t1(a, off=off, wsc=wsc, q1=q1):
            return a.activation(q1, WST[:, off:off + 1024],
                                AF.Copy, bias=C23, scale=wsc)
        P.op(ac, a_t1, reads=(r_wstB, r_cols), writes=rq1, tag="t1")
        P.fence(ac)
        def g_t3(g, off=off, wsck=wsck, q3=q3):
            return g.tensor_scalar(q3, WST[:, off:off + 1024],
                                   wsck, C52, op0=ALU.mult, op1=ALU.add)
        P.op(gp, g_t3, reads=(r_wstB, r_cols), writes=rq3, tag="t3")
        P.fence(gp)
        def d_q(v, i=i, q1=q1, q3=q3):
            v.tensor_scalar(q1, q1, C23, col("om"),
                            op0=ALU.subtract, op1=ALU.mult)
            v.scalar_tensor_tensor(q1, q3, C52, q1,
                                   op0=ALU.subtract, op1=ALU.add)
            wslot = (i % 2) * 1024
            return v.tensor_scalar(WQ3[:, wslot:wslot + 1024],
                                   q1, 1.0, -1.0, op0=ALU.min, op1=ALU.max)
        P.op(dv, d_q, reads=rq1 + rq3 + (r_cols,),
             writes=rq1 + rq3 + (r_wq3,), tag="q3")
        P.fence(dv)

    def tern_B_back(r_dst, dst_fn, i):
        q1, q3, rq1, rq3, pb, rpb = tern_B_parity(i)
        wslot = (i % 2) * 1024
        for b in range(8):
            def tr(t, b=b, wslot=wslot, pb=pb):
                return t.transpose(pb[:, b * 128:(b + 1) * 128],
                                   WQ3[:, wslot + b * 128:wslot + (b + 1) * 128],
                                   IDN[:, :])
            P.op(pe, tr, reads=(r_wq3, r_idn), writes=(rpb,), tag="wtr")
        P.fence(pe)
        def a_ev(a, i=i, pb=pb):
            return a.copy(dst_fn(i), pb[:, :].rearrange("p (k n) -> p k n", k=8))
        P.op(ac, a_ev, reads=(rpb,), writes=(r_dst,), tag="wev")
        P.fence(ac)

    TERN = [
        ("g", lambda i: wg_h[i * 128:(i + 1) * 128, :], r_wgT,
         lambda i: wgT.ap().rearrange("p (k n) -> p k n", k=8)[:, :, i * 128:(i + 1) * 128]),
        ("u", lambda i: wu_h[i * 128:(i + 1) * 128, :], r_wuT,
         lambda i: wuT.ap().rearrange("p (k n) -> p k n", k=8)[:, :, i * 128:(i + 1) * 128]),
        ("d", lambda i: wd_h[(i // 4) * 128:(i // 4 + 1) * 128,
                             (i % 4) * 1024:(i % 4 + 1) * 1024], r_wdT,
         lambda i: wdT.ap().rearrange("p (k n) -> p k n", k=32)
         [:, (i % 4) * 8:(i % 4) * 8 + 8, (i // 4) * 128:(i // 4 + 1) * 128]),
    ]
    for i in range(32):
        tern_A_tile(TERN[0][0], TERN[0][1], i)
    tern_scale(TERN[0][0])
    for k in range(3):
        wname, src_fn, r_dst, dst_fn = TERN[k]
        for i in range(32):
            tern_B_front(wname, src_fn, i)
            if k + 1 < 3:
                tern_A_tile(TERN[k + 1][0], TERN[k + 1][1], i)
            if i > 0:
                tern_B_back(r_dst, dst_fn, i - 1)
        tern_B_back(r_dst, dst_fn, 31)
        if k + 1 < 3:
            tern_scale(TERN[k + 1][0])
    P.handoff([r_x1], [r_tb1, r_tb3])

    dbg("wgt", wgT[:, 0:4096], [r_wgT])
    dbg("wdt", wdT[:, 0:4096], [r_wdT])

    def ln_stats(src_res, src_ap, pref):
        mcol, rcol = col("m_" + pref), col("r_" + pref)
        def d_st(v, src_ap=src_ap):
            v.bn_stats(STAT[:, 0:6], src_ap[:, 0:512])
            v.bn_stats(STAT[:, 6:12], src_ap[:, 512:1024])
            v.drain()
            v.bn_aggr(STAT[:, 12:14], STAT[:, 0:12])
            v.drain()
            v.tensor_copy(mcol, STAT[:, 12:13])
            return v.tensor_scalar(col("tv"), STAT[:, 13:14], EPS, None, op0=ALU.add)
        P.op(dv, d_st, reads=(src_res,), writes=(r_stat, r_cols), tag="lnst_" + pref)
        P.fence(dv)
        def a_r(a):
            a.activation(col("tv2"), col("tv"), AF.Ln)
            a.drain()
            return a.activation(rcol, col("tv2"), AF.Exp, scale=-0.5)
        P.op(ac, a_r, reads=(r_cols,), writes=(r_cols,), tag="lnr_" + pref)
        P.fence(ac)
        return mcol, rcol

    _trot = [0]

    def transpose_group(src_ap_fn, n, pbank_ignored, r_ignored, dst_ap, r_src, r_dst,
                        tag):
        rot = _trot[0]
        _trot[0] += 1
        pbank, r_pbank = (pT0, r_pT0) if rot % 2 == 0 else (pT1, r_pT1)
        for b in range(n):
            def tr(t, b=b, pbank=pbank):
                return t.transpose(pbank[:, b * 128:(b + 1) * 128], src_ap_fn(b), IDN[:, :])
            P.op(pe, tr, reads=(r_src, r_idn), writes=(r_pbank,), tag="tr_" + tag)
        P.fence(pe)
        eng = dv if (rot // 2) % 2 == 0 else ac
        def ev(e, pbank=pbank):
            return e.tensor_copy(dst_ap,
                                 pbank[:, 0:n * 128].rearrange("p (k n) -> p k n", k=n))
        if eng is dv:
            P.op(dv, ev, reads=(r_pbank,), writes=(r_dst,), tag="ev_" + tag)
            P.fence(dv)
        else:
            P.op(ac, lambda a, pbank=pbank: a.copy(
                dst_ap, pbank[:, 0:n * 128].rearrange("p (k n) -> p k n", k=n)),
                reads=(r_pbank,), writes=(r_dst,), tag="ev_" + tag)
            P.fence(ac)

    def xbar_T(in_ap, out_ap, r_src, r_dst, tag):
        def tr(sq):
            return sq.dma_start(out=out_ap, in_=in_ap, transpose=True)
        P.op(d_t, tr, reads=(r_src,), writes=(r_dst,), tag="xT_" + tag)
        P.fence(d_t)

    U_AP = SCR[:, 0:1024]
    T1_AP = SCR[:, 1024:2048]
    SILU_AP = (SC2[:, 0:512], SC2[:, 512:1024])
    SILU_RES = (r_s2b0, r_s2b1)

    def x_quant(pref, ap_, par, tau, in_buf, r_in, t3ap, r_t3):
        mcol = col(f"m_{pref}{tau}_{par}")
        rcol = col(f"r_{pref}{tau}_{par}")
        def d_u(v, tau=tau, mcol=mcol, ap_=ap_):
            v.tensor_scalar(U_AP, in_buf[:, tau * 1024:(tau + 1) * 1024], mcol,
                            None, op0=ALU.subtract)
            return v.reduce_max(col(f"mxm{tau}_{ap_}"), U_AP, axis=AX,
                                apply_absolute_value=True)
        P.op(dv, d_u, reads=(r_in, r_cols), writes=(r_u, r_cols), tag="u")
        def d_sc(v, tau=tau, rcol=rcol, ap_=ap_):
            q = f"{tau}_{ap_}"
            v.drain()
            v.tensor_scalar(col(f"mcl{q}"), col(f"mxm{q}"), 1e-5, None, op0=ALU.max)
            v.tensor_mul(col(f"rmx{q}"), rcol, col(f"mxm{q}"))
            v.drain()
            v.reciprocal(col(f"srec{q}"), col(f"mcl{q}"))
            v.tensor_scalar(col(f"rmx{q}"), col(f"rmx{q}"), 1e-5, None, op0=ALU.max)
            v.drain()
            v.tensor_scalar(col(f"s{q}"), col(f"srec{q}"), 127.0, None, op0=ALU.mult)
            v.tensor_mul(col(f"cg{q}"), col(f"rmx{q}"), col("swrec_g"))
            v.tensor_mul(col(f"cu{q}"), col(f"rmx{q}"), col("swrec_u"))
            v.drain()
            v.tensor_mul(col(f"skm{q}"), col(f"s{q}"), col("flag"))
            v.tensor_scalar(col(f"cg{q}"), col(f"cg{q}"), 1.0 / 127.0, None,
                            op0=ALU.mult)
            return v.tensor_scalar(col(f"cu{q}"), col(f"cu{q}"), 1.0 / 127.0,
                                   None, op0=ALU.mult)
        P.op(dv, d_sc, reads=(r_cols,), writes=(r_cols,), tag="sc")
        P.fence(dv)
        def a_t1(a, tau=tau, ap_=ap_):
            return a.activation(T1_AP, U_AP, AF.Copy, bias=C23,
                                scale=col(f"s{tau}_{ap_}"))
        P.op(ac, a_t1, reads=(r_u, r_cols), writes=(r_t1,), tag="xt1")
        P.fence(ac)
        def g_t3(g, tau=tau, ap_=ap_, t3ap=t3ap):
            return g.tensor_scalar(t3ap, U_AP, col(f"skm{tau}_{ap_}"), C52,
                                   op0=ALU.mult, op1=ALU.add)
        P.op(gp, g_t3, reads=(r_u, r_cols), writes=(r_t3,), tag="xt3")
        P.fence(gp)
        def a_t2(a):
            a.activation(T1_AP, T1_AP, AF.Copy, bias=-C23, scale=1.0)
            return a.activation(T1_AP, T1_AP, AF.Copy, bias=0.0,
                                scale=col("om"))
        P.op(ac, a_t2, reads=(r_t1, r_cols), writes=(r_t1,), tag="xt2")
        P.fence(ac)
        def d_x(v, t3ap=t3ap):
            return v.scalar_tensor_tensor(XQ[:, :], t3ap, C52, T1_AP,
                                          op0=ALU.subtract, op1=ALU.add)
        P.op(dv, d_x, reads=(r_t1, r_t3, r_cols), writes=(r_xq,), tag="xq")
        P.fence(dv)

    def x_quant_T(tau):
        if tau == 0:
            dst = XQF.ap().rearrange("p (k n) -> p k n", k=8)
            rdst = r_xqf
        else:
            dst = FMP[:, 0:1024].rearrange("p (k n) -> p k n", k=8)
            rdst = r_fmpq[0]
        xbar_T(XQ.ap().rearrange("p (k n) -> p k n", k=8), dst, r_xq, rdst, "xq")

    FFN_COL_NAMES = ("mxm", "mcl", "srec", "s", "skm", "rmx", "cg", "cu",
                     "mxh", "t2c", "s2r", "s2", "c2", "sig", "sigk")

    def reg_ffn_cols(ap_):
        for t_ in range(2):
            for n_ in FFN_COL_NAMES:
                col(f"{n_}{t_}_{ap_}")
            for sl_ in range(8):
                col(f"hmx{sl_}_{t_}_{ap_}")

    def ffn(c, in_buf, r_in, out_buf, r_out, scr_buf, r_scrb, pref,
            xpath_done=False, tail_a=None, tail_b=None, tail_gu=None):
        ap_ = 0 if pref == "f1" else 1
        reg_ffn_cols(ap_)
        dsts = FFN_SIDE if not xpath_done else \
            [r for r in FFN_SIDE if r not in tuple(r_fmpq) + (r_t1, r_u)]
        P.handoff(dsts, ATTN_SIDE + [r_wst, r_wstA, r_wstB, r_wq3, r_rows, r_scr])
        par = c % 2

        def emit_xpath(tau):
            x_quant(pref, ap_, par, tau, in_buf, r_in,
                    scr_buf[:, tau * 1024:(tau + 1) * 1024], r_scrb)

        def emit_xqT(tau):
            x_quant_T(tau)

        def lhsT_ap(tau, K):
            src = XQF if tau == 0 else FMP
            return src[:, K * 128:(K + 1) * 128]

        GU_BANKS = [(pW0, r_pW0), (pW1, r_pW1), (pW2, r_pW2)]

        def emit_gu_slice(tau, sl, nb):
            banks = GU_BANKS
            rsrc = r_xqf if tau == 0 else r_fmpq[0]
            if True:
                bg, rbg = banks[nb % 3]; nb += 1
                bu, rbu = banks[nb % 3]; nb += 1
                for K in range(8):
                    def mmg(t, K=K, sl=sl, tau=tau, bg=bg):
                        return t.matmul(bg[:, :], lhsT_ap(tau, K),
                                        wgT[:, K * INNER + sl * 512:K * INNER + (sl + 1) * 512],
                                        start=(K == 0), stop=(K == 7))
                    P.op(pe, mmg, reads=(rsrc, r_wgT), writes=(rbg,), tag="mmg")
                P.fence(pe)
                for K in range(8):
                    def mmu(t, K=K, sl=sl, tau=tau, bu=bu):
                        return t.matmul(bu[:, :], lhsT_ap(tau, K),
                                        wuT[:, K * INNER + sl * 512:K * INNER + (sl + 1) * 512],
                                        start=(K == 0), stop=(K == 7))
                    P.op(pe, mmu, reads=(rsrc, r_wuT), writes=(rbu,), tag="mmu")
                P.fence(pe)
                sap, sres = SILU_AP[sl % 2], SILU_RES[sl % 2]
                def a_silu(a, tau=tau, bg=bg, sap=sap, ap_=ap_):
                    return a.activation(sap, bg[:, :], AF.Silu,
                                        scale=col(f"cg{tau}_{ap_}"))
                P.op(ac, a_silu, reads=(rbg, r_cols), writes=(sres,), tag="silu")
                P.fence(ac)
                def d_h(v, sl=sl, tau=tau, bu=bu, sap=sap, ap_=ap_):
                    v.tensor_mul(HT[:, sl * 512:(sl + 1) * 512], sap, bu[:, :])
                    return v.reduce_max(col(f"hmx{sl}_{tau}_{ap_}"),
                                        HT[:, sl * 512:(sl + 1) * 512], axis=AX,
                                        apply_absolute_value=True)
                P.op(dv, d_h, reads=(sres, rbu), writes=(r_ht, r_cols), tag="h")
                P.fence(dv)
            return nb

        def emit_s2chain(tau):
            q = f"{tau}_{ap_}"
            c0 = col_ids[f"hmx0_{q}"]
            assert col_ids[f"hmx7_{q}"] == c0 + 7
            def d_s2(v, q=q, c0=c0):
                v.drain()
                v.reduce_max(col(f"mxh{q}"), COLS[:, c0:c0 + 8], axis=AX)
                v.drain()
                v.tensor_mul(col(f"t2c{q}"), col(f"cu{q}"), col(f"mxh{q}"))
                v.drain()
                v.tensor_scalar(col(f"t2c{q}"), col(f"t2c{q}"), 1e-5, None,
                                op0=ALU.max)
                v.drain()
                v.reciprocal(col(f"s2r{q}"), col(f"t2c{q}"))
                v.tensor_mul(col(f"c2{q}"), col(f"t2c{q}"), col("swrec_d"))
                v.drain()
                v.tensor_scalar(col(f"s2{q}"), col(f"s2r{q}"), 127.0, None,
                                op0=ALU.mult)
                v.tensor_scalar(col(f"c2{q}"), col(f"c2{q}"), 0.5 / 127.0, None,
                                op0=ALU.mult)
                v.drain()
                v.tensor_mul(col(f"sig{q}"), col(f"s2{q}"), col(f"cu{q}"))
                v.drain()
                return v.tensor_mul(col(f"sigk{q}"), col(f"sig{q}"), col("flag"))
            P.op(dv, d_s2, reads=(r_cols,), writes=(r_cols,), tag="s2")
            P.fence(dv)

        def emit_hqq_T(tau, q):
            xbar_T(HQ[:, q * 1024:(q + 1) * 1024].rearrange("p (k n) -> p k n", k=8),
                   FMP[:, q * 1024:(q + 1) * 1024].rearrange("p (k n) -> p k n", k=8),
                   r_hqq[q], r_fmpq[q], f"hq{q}")

        def emit_round_quarter(tau, q, defer_dma=False):
            t3reg = scr_buf[:, 1024:2048]
            if True:
                hsl = HT[:, q * 1024:(q + 1) * 1024]
                def a_h1(a, tau=tau, hsl=hsl, ap_=ap_):
                    return a.activation(T1_AP, hsl, AF.Copy, bias=C23,
                                        scale=col(f"sig{tau}_{ap_}"))
                P.op(ac, a_h1, reads=(r_ht, r_cols), writes=(r_t1,), tag="h1")
                P.fence(ac)
                def g_h3(g, tau=tau, hsl=hsl, t3reg=t3reg, ap_=ap_):
                    return g.tensor_scalar(t3reg, hsl, col(f"sigk{tau}_{ap_}"), C52,
                                           op0=ALU.mult, op1=ALU.add)
                P.op(gp, g_h3, reads=(r_ht, r_cols), writes=(r_scrb,), tag="h3")
                P.fence(gp)
                def a_h2(a):
                    a.activation(T1_AP, T1_AP, AF.Copy, bias=-C23, scale=1.0)
                    return a.activation(T1_AP, T1_AP, AF.Copy, bias=0.0,
                                        scale=col("om"))
                P.op(ac, a_h2, reads=(r_t1, r_cols), writes=(r_t1,), tag="h2")
                P.fence(ac)
                def d_hq(v, q=q, t3reg=t3reg):
                    return v.scalar_tensor_tensor(HQ[:, q * 1024:(q + 1) * 1024],
                                                  t3reg, C52, T1_AP,
                                                  op0=ALU.subtract, op1=ALU.add)
                P.op(dv, d_hq, reads=(r_t1, r_scrb, r_cols),
                     writes=(r_hqq[q],), tag="hq")
                P.fence(dv)
            if not defer_dma:
                emit_hqq_T(tau, q)

        DOWN_KORDER = list(range(8, 32)) + list(range(8))

        def emit_down(tau):
            for idx, K in enumerate(DOWN_KORDER):
                for half in range(2):
                    def mmd(t, K=K, half=half, idx=idx):
                        return t.matmul(pD[:, half * 512:(half + 1) * 512],
                                        FMP[:, K * 128:(K + 1) * 128],
                                        wdT[:, K * 1024 + half * 512:K * 1024 + (half + 1) * 512],
                                        start=(idx == 0), stop=(idx == 31))
                    P.op(pe, mmd, reads=(r_fmpq[K // 8], r_wdT), writes=(r_pD,), tag="mmd")
            P.fence(pe)

        def emit_downevac(tau):
            to = tau * 1024
            for half in range(2):
                def d_out(v, to=to, half=half, tau=tau, ap_=ap_):
                    return v.scalar_tensor_tensor(
                        out_buf[:, to + half * 512:to + (half + 1) * 512],
                        pD[:, half * 512:(half + 1) * 512], col(f"c2{tau}_{ap_}"),
                        in_buf[:, to + half * 512:to + (half + 1) * 512],
                        op0=ALU.mult, op1=ALU.add)
                P.op(dv, d_out, reads=(r_pD, r_in, r_cols), writes=(r_out,), tag="dout")
                P.fence(dv)

        if not xpath_done:
            emit_xpath(0)
            emit_xqT(0)
            emit_xpath(1)
        nb = 0
        for sl in range(8):
            nb = emit_gu_slice(0, sl, nb)
        if not xpath_done:
            emit_xqT(1)
        emit_s2chain(0)
        nb = 0
        for sl in range(8):
            if sl % 2 == 0:
                emit_round_quarter(0, sl // 2, defer_dma=(sl == 0))
            nb = emit_gu_slice(1, sl, nb)
        if tail_gu is not None:
            tail_gu()
        emit_hqq_T(0, 0)
        emit_s2chain(1)
        emit_down(0)
        for q in range(4):
            emit_round_quarter(1, q)
        if tail_a is not None:
            tail_a()
        emit_downevac(0)
        emit_down(1)
        if tail_b is not None:
            tail_b()
        emit_downevac(1)

    def attn_ln(c, tau):
        mcol, rcol = ln_stats(r_x1, X1[:, tau * 1024:(tau + 1) * 1024], f"at{tau}")
        def d_ln(v, tau=tau, mcol=mcol, rcol=rcol):
            return v.tensor_scalar(LNB[:, tau * 1024:(tau + 1) * 1024],
                                   X1[:, tau * 1024:(tau + 1) * 1024],
                                   mcol, rcol, op0=ALU.subtract, op1=ALU.mult)
        P.op(dv, d_ln, reads=(r_x1, r_cols), writes=(r_lnb, r_hqq[tau]), tag="lnA")
        P.fence(dv)

    def attn(c, ln_done=False):
        dsts = ATTN_SIDE if not ln_done else [r for r in ATTN_SIDE if r is not r_lnb]
        P.handoff(dsts, FFN_SIDE + [r_rows, r_xq, r_wq3])
        par = c % 2
        def d_zm(v):
            for m_ in range(4):
                v.tensor_copy(MEMZ[:, m_ * 65 + 64:m_ * 65 + 65], ZS[:, m_:m_ + 1])
            return v.nop()
        P.op(dv, d_zm, reads=(r_zs,), writes=(r_memz,), tag="zmem")
        P.fence(dv)
        for tau in range(2):
            if not ln_done:
                attn_ln(c, tau)
            xbar_T(LNB[:, tau * 1024:(tau + 1) * 1024].rearrange("p (k n) -> p k n", k=8),
                   LNF.rearrange("p (k n) -> p k n", k=8)[:, :, tau * 128:(tau + 1) * 128],
                   r_lnb, r_lnf, f"lnf{tau}")
        for m in range(4):
            bank, rb = (pW0, r_pW0) if m < 2 else (pW1, r_pW1)
            boff = (m % 2) * 256
            for K in range(8):
                def mmq(t, m=m, K=K, bank=bank, boff=boff):
                    return t.matmul(bank[:, boff:boff + 256],
                                    wqb[:, K * 512 + m * 128:K * 512 + (m + 1) * 128],
                                    LNF[:, K * 256:(K + 1) * 256],
                                    start=(K == 0), stop=(K == 7))
                P.op(pe, mmq, reads=(r_awt, r_lnf), writes=(rb,), tag="mmq")
        P.fence(pe)
        for m in range(4):
            for K in range(8):
                def mmk(t, m=m, K=K):
                    return t.matmul(pD[:, m * 256:(m + 1) * 256],
                                    wkb[:, K * 512 + m * 128:K * 512 + (m + 1) * 128],
                                    LNF[:, K * 256:(K + 1) * 256],
                                    start=(K == 0), stop=(K == 7))
                P.op(pe, mmk, reads=(r_awt, r_lnf), writes=(r_pD,), tag="mmk")
        P.fence(pe)
        for m in range(4):
            bank = pW0 if m < 2 else pW1
            rb = r_pW0 if m < 2 else r_pW1
            boff = (m % 2) * 256
            def a_eq(a, m=m, bank=bank, boff=boff):
                return a.activation(SC2[:, m * 256:(m + 1) * 256], bank[:, boff:boff + 256],
                                    AF.Exp)
            P.op(ac, a_eq, reads=(rb,), writes=(r_sc2,), tag="eq")
            P.fence(ac)
            def a_qc(a, m=m, bank=bank, boff=boff):
                return a.activation(QB[:, m * 256:(m + 1) * 256], bank[:, boff:boff + 256],
                                    AF.Copy, scale=0.125)
            P.op(ac, a_qc, reads=(rb,), writes=(r_qb,), tag="qc")
            P.fence(ac)
            def d_q(v, m=m, bank=bank, boff=boff):
                v.tensor_scalar(SC2[:, m * 256:(m + 1) * 256], SC2[:, m * 256:(m + 1) * 256],
                                1.0, None, op0=ALU.min)
                return v.scalar_tensor_tensor(SQF[:, m * 256:(m + 1) * 256],
                                              bank[:, boff:boff + 256], 0.0,
                                              SC2[:, m * 256:(m + 1) * 256],
                                              op0=ALU.max, op1=ALU.add)
            P.op(dv, d_q, reads=(rb, r_sc2), writes=(r_sqf, r_sc2), tag="dq")
            P.fence(dv)
        if c == 0:
            dbg("lnb", LNB, [r_lnb])
            dbg("qb", QB, [r_qb])
            dbg("sqf", SQF, [r_sqf])
        for m in range(4):
            def a_ek(a, m=m):
                return a.activation(SC2[:, m * 256:(m + 1) * 256], pD[:, m * 256:(m + 1) * 256],
                                    AF.Exp)
            P.op(ac, a_ek, reads=(r_pD,), writes=(r_sc2,), tag="ek")
            P.fence(ac)
            def a_kc(a, m=m):
                return a.activation(KB[:, m * 256:(m + 1) * 256],
                                    pD[:, m * 256:(m + 1) * 256], AF.Copy)
            P.op(ac, a_kc, reads=(r_pD,), writes=(r_kb,), tag="kc")
            P.fence(ac)
            def d_k(v, m=m):
                v.tensor_scalar(SC2[:, m * 256:(m + 1) * 256], SC2[:, m * 256:(m + 1) * 256],
                                1.0, None, op0=ALU.min)
                v.scalar_tensor_tensor(SKF[:, m * 256:(m + 1) * 256],
                                       pD[:, m * 256:(m + 1) * 256], 0.0,
                                       SC2[:, m * 256:(m + 1) * 256],
                                       op0=ALU.max, op1=ALU.add,
                                       accum_out=col(f"zd{m}"))
                v.drain()
                return v.tensor_add(ZS[:, m:m + 1], ZS[:, m:m + 1], col(f"zd{m}"))
            P.op(dv, d_k, reads=(r_pD, r_sc2), writes=(r_skf, r_sc2, r_zs, r_cols),
                 tag="dk")
            P.fence(dv)
        if c == 0:
            dbg("kb", KB, [r_kb])
            dbg("skf", SKF, [r_skf])
            dbg("zs", ZS[:, :], [r_zs])
        for tau in range(2):
            bank, rb = (pW0, r_pW0) if tau == 0 else (pW1, r_pW1)
            for K in range(8):
                def mmv(t, tau=tau, K=K, bank=bank):
                    return t.matmul(bank[:, :],
                                    LNF[:, K * 256 + tau * 128:K * 256 + (tau + 1) * 128],
                                    wvb[:, K * 512:(K + 1) * 512],
                                    start=(K == 0), stop=(K == 7))
                P.op(pe, mmv, reads=(r_awt, r_lnf), writes=(rb,), tag="mmv")
            P.fence(pe)
            def a_vc(a, tau=tau, bank=bank):
                return a.activation(VT[:, tau * 512:(tau + 1) * 512], bank[:, :], AF.Copy)
            P.op(ac, a_vc, reads=(rb,), writes=(r_vt,), tag="vc")
            P.fence(ac)
            def d_v(v, tau=tau, bank=bank):
                return v.tensor_mul(VP[:, tau * 512:(tau + 1) * 512], bank[:, :], OGF[:, :])
            P.op(dv, d_v, reads=(rb, r_ogf), writes=(r_vp,), tag="dv")
            P.fence(dv)
        if c == 0:
            dbg("vt", VT, [r_vt])
            dbg("vp", VP, [r_vp])
        for m in range(4):
            pb, rpb = (pT0, r_pT0) if m < 2 else (pT1, r_pT1)
            for b in range(2):
                def tr(t, m=m, b=b, pb=pb):
                    return t.transpose(pb[:, (2 * (m % 2) + b) * 128:(2 * (m % 2) + b + 1) * 128],
                                       SKF[:, m * 256 + b * 128:m * 256 + (b + 1) * 128],
                                       IDN[:, :])
                P.op(pe, tr, reads=(r_skf, r_idn), writes=(rpb,), tag="sktr")
            if m % 2 == 1:
                P.fence(pe)
                def ev(v, m=m, pb=pb):
                    return v.tensor_copy(
                        SKT.rearrange("p (t q n) -> p t q n", t=2, q=4)
                        [:, :, m - 1:m + 1, :],
                        pb[:, 0:512].rearrange("p (q t n) -> p t q n", q=2, t=2))
                P.op(dv, ev, reads=(rpb,), writes=(r_skt,), tag="skev")
                P.fence(dv)
        for m in range(4):
            for tau in range(2):
                def mmm(t, m=m, tau=tau, c=c):
                    return t.matmul(pM[:, m * 128:(m + 1) * 128],
                                    SKT[:, tau * 512 + m * 128:tau * 512 + (m + 1) * 128],
                                    VT[:, tau * 512 + m * 128:tau * 512 + (m + 1) * 128],
                                    start=(c == 0 and m == 0 and tau == 0), stop=False)
                P.op(pe, mmm, reads=(r_skt, r_vt), writes=(r_pM,), tag="mmm")
        P.fence(pe)
        for hh_ in range(2):
            for i_ in range(2):
                for n_ in ("rd", "rs"):
                    col(f"{n_}{hh_}{i_}")
                for p_ in range(2):
                    for n_ in ("sb", "se"):
                        col(f"{n_}{hh_}{i_}_{p_}")

        def eb_of(m):
            return (EB0, r_eb) if m % 2 == 0 else (EB1, r_eb1)

        def scores_banks(m):
            if m % 2 == 0:
                return ((pW0, r_pW0), (pW1, r_pW1))
            return ((pD[:, 0:512], r_pD), (pD[:, 512:1024], r_pD))

        def emit_scores(m):
            for hh in range(2):
                bank, rb = scores_banks(m)[hh]
                for i in range(2):
                    def mms(t, m=m, hh=hh, i=i, bank=bank):
                        return t.matmul(bank[:, i * 256:(i + 1) * 256],
                                        QB[:, m * 256 + i * 128:m * 256 + (i + 1) * 128]
                                        [hh * 64:(hh + 1) * 64, :],
                                        KB[:, m * 256:(m + 1) * 256][hh * 64:(hh + 1) * 64, :],
                                        start=True, stop=True)
                    P.op(pe, mms, reads=(r_qb, r_kb), writes=(rb,), tag="mms")
            P.fence(pe)

        def emit_softmax(m):
            for hh in range(2):
                bank, rb = scores_banks(m)[hh]
                for i in range(2):
                    def d_mx(v, hh=hh, i=i, bank=bank, m=m):
                        return v.reduce_max(col(f"sb{hh}{i}_{m % 2}"),
                                            bank[:, i * 256:(i + 1) * 256],
                                            axis=AX, negate=True)
                    P.op(dv, d_mx, reads=(rb,), writes=(r_cols,), tag="smx")
                    P.fence(dv)
                    ebap, rebm = eb_of(m)
                    wr = (rebm, r_cols) if m % 2 == 0 else (rebm, r_lnb, r_cols)
                    def a_e(a, hh=hh, i=i, bank=bank, m=m, ebap=ebap):
                        return a.activation(ebap[:, (hh * 2 + i) * 256:(hh * 2 + i + 1) * 256],
                                            bank[:, i * 256:(i + 1) * 256], AF.Exp,
                                            bias=col(f"sb{hh}{i}_{m % 2}"),
                                            accum_out=col(f"se{hh}{i}_{m % 2}"))
                    P.op(ac, a_e, reads=(rb, r_cols), writes=wr, tag="exp")
                    P.fence(ac)

        def emit_ET(m):
            ebap, rebm = eb_of(m)
            for hh in range(2):
                dst = ETB if hh == 0 else ETB1
                xbar_T(ebap[:, hh * 512:(hh + 1) * 512].rearrange("p (k n) -> p k n", k=4),
                       dst.rearrange("p (k n) -> p k n", k=4), rebm, r_etb, "et")

        def emit_admd(m):
            for hh in range(2):
                bank, rb = scores_banks(m)[hh]
                h = 2 * m + hh
                src = ETB if hh == 0 else ETB1
                for i in range(2):
                    for jb in range(2):
                        def mmad(t, i=i, jb=jb, h=h, bank=bank, src=src):
                            return t.matmul(bank[:, i * 64:(i + 1) * 64],
                                            src[:, (i * 2 + jb) * 128:(i * 2 + jb + 1) * 128],
                                            VP[:, jb * 512 + h * 64:jb * 512 + h * 64 + 64],
                                            start=(jb == 0), stop=(jb == 1))
                        P.op(pe, mmad, reads=(r_etb, r_vp), writes=(rb,), tag="mmad")
                for i in range(2):
                    def mmam(t, i=i, m=m, hh=hh, bank=bank):
                        return t.matmul(bank[:, 128 + i * 65:128 + i * 65 + 65],
                                        SQF[:, m * 256 + i * 128:m * 256 + (i + 1) * 128]
                                        [hh * 64:(hh + 1) * 64, :],
                                        MEMZ[hh * 64:(hh + 1) * 64, m * 65:m * 65 + 65],
                                        start=True, stop=True)
                    P.op(pe, mmam, reads=(r_sqf, r_memz), writes=(rb,), tag="mmam")
            P.fence(pe)

        def emit_combine(m):
            for hh in range(2):
                bank, rb = scores_banks(m)[hh]
                h = 2 * m + hh
                for i in range(2):
                    def d_cb(v, i=i, hh=hh, h=h, bank=bank, m=m):
                        v.reciprocal(col(f"rs{hh}{i}"), col(f"se{hh}{i}_{m % 2}"))
                        v.reciprocal(col(f"rd{hh}{i}"),
                                     bank[:, 128 + i * 65 + 64:128 + i * 65 + 65])
                        v.tensor_scalar(SC2[:, 0:64], bank[:, i * 64:(i + 1) * 64],
                                        col(f"rs{hh}{i}"), None, op0=ALU.mult)
                        return v.scalar_tensor_tensor(
                            ATT[:, i * 512 + h * 64:i * 512 + (h + 1) * 64],
                            bank[:, 128 + i * 65:128 + i * 65 + 64], col(f"rd{hh}{i}"),
                            SC2[:, 0:64], op0=ALU.mult, op1=ALU.add)
                    P.op(dv, d_cb, reads=(rb, r_cols), writes=(r_att, r_sc2, r_cols),
                         tag="cb")
                    P.fence(dv)

        emit_scores(0)
        emit_softmax(0)
        for m in range(4):
            if m < 3:
                emit_scores(m + 1)
                emit_softmax(m + 1)
            emit_ET(m)
            emit_admd(m)
            emit_combine(m)
        for m in range(4):
            for parh in range(2):
                def d_mz(v, m=m, parh=parh):
                    return v.tensor_mul(
                        MEMZ[parh * 64:(parh + 1) * 64, m * 65:m * 65 + 64],
                        pM[parh * 64:(parh + 1) * 64,
                           m * 128 + parh * 64:m * 128 + parh * 64 + 64],
                        GB[parh * 64:(parh + 1) * 64, m * 64:(m + 1) * 64])
                P.op(dv, d_mz, reads=(r_pM, r_gb), writes=(r_memz,), tag="mz")
        P.fence(dv)
        if c == 0:
            dbg("att", ATT, [r_att])
            dbg("memz1", MEMZ[:, :], [r_memz])
        for i in range(2):
            xbar_T(ATT[:, i * 512:(i + 1) * 512].rearrange("p (k n) -> p k n", k=4),
                   ATF.rearrange("p (k n) -> p k n", k=4)[:, :, i * 128:(i + 1) * 128],
                   r_att, r_atf, "at")
        for tau in range(2):
            for half in range(2):
                for K in range(4):
                    def mmo(t, tau=tau, half=half, K=K):
                        return t.matmul(pD[:, half * 512:(half + 1) * 512],
                                        ATF[:, K * 256 + tau * 128:K * 256 + (tau + 1) * 128],
                                        wob[:, K * 1024 + half * 512:K * 1024 + (half + 1) * 512],
                                        start=(K == 0), stop=(K == 3))
                    P.op(pe, mmo, reads=(r_atf, r_awt), writes=(r_pD,), tag="mmo")
                P.fence(pe)
                def d_x2(v, tau=tau, half=half):
                    return v.tensor_add(X2[:, tau * 1024 + half * 512:tau * 1024 + (half + 1) * 512],
                                        X1[:, tau * 1024 + half * 512:tau * 1024 + (half + 1) * 512],
                                        pD[:, half * 512:(half + 1) * 512])
                P.op(dv, d_x2, reads=(r_pD, r_x1), writes=(r_x2,), tag="dx2")
                P.fence(dv)
        if c == 0:
            dbg("x2c0", X2[:, :], [r_x2])

    for tau in range(2):
        m0, r0 = ln_stats(r_xin, XIN[:, tau * 1024:(tau + 1) * 1024], f"f1{tau}_0")

    reg_ffn_cols(0)
    x_quant("f1", 0, 0, 0, XIN, r_xin, X2[:, 0:1024], r_x2)
    x_quant_T(0)
    x_quant("f1", 0, 0, 1, XIN, r_xin, X2[:, 1024:2048], r_x2)
    x_quant_T(1)

    for c in range(NCH):
        par = c % 2
        tb_f1 = (lambda c=c: attn_ln(c, 0)) if stage >= 2 else None
        ffn(c, XIN, r_xin, X1, r_x1, X2, r_x2, "f1", xpath_done=True, tail_b=tb_f1)
        if c + 1 < NCH:
            x_load(c + 1)
        if stage == 1:
            src, r_src = X1, r_x1
        else:
            attn_ln(c, 1)
            attn(c, ln_done=True)
            if c + 1 < NCH:
                for tau in range(2):
                    ln_stats(r_xin, XIN[:, tau * 1024:(tau + 1) * 1024],
                             f"f1{tau}_{1 - par}")
            if stage == 2:
                src, r_src = X2, r_x2
            else:
                for tau in range(2):
                    ln_stats(r_x2, X2[:, tau * 1024:(tau + 1) * 1024], f"f2{tau}_{par}")
                reg_ffn_cols(1)
                x_quant("f2", 1, par, 0, X2, r_x2, X1[:, 0:1024], r_x1)
                x_quant_T(0)
                x_quant("f2", 1, par, 1, X2, r_x2, X1[:, 1024:2048], r_x1)
                x_quant_T(1)
                if c + 1 < NCH:
                    def ta_f2(c=c, par=par):
                        x_quant("f1", 0, 1 - par, 0, XIN, r_xin, X1[:, 0:1024], r_x1)
                        x_quant_T(0)
                        x_quant("f1", 0, 1 - par, 1, XIN, r_xin, X1[:, 0:1024], r_x1)
                    def tb_f2(c=c):
                        x_quant_T(1)
                else:
                    ta_f2 = tb_f2 = None
                ffn(c, X2, r_x2, X1, r_x1, X1, r_x1, "f2", xpath_done=True,
                    tail_a=ta_f2, tail_b=tb_f2)
                src, r_src = X1, r_x1
        def st(s, c=c, src=src):
            return s.dma_start(
                out=y_h[c * 256:(c + 1) * 256, :].rearrange("(t p) n -> p t n", p=128),
                in_=src.ap().rearrange("p (t n) -> p t n", t=2))
        P.op(d_o, st, reads=(r_src,), writes=(), tag=f"store{c}")
        P.fence(d_o)
        if stage == 1 and c + 1 < NCH:
            for tau in range(2):
                ln_stats(r_xin, XIN[:, tau * 1024:(tau + 1) * 1024],
                         f"f1{tau}_{1 - par}")
            x_quant("f1", 0, 1 - par, 0, XIN, r_xin, X2[:, 0:1024], r_x2)
            x_quant_T(0)
            x_quant("f1", 0, 1 - par, 1, XIN, r_xin, X2[:, 1024:2048], r_x2)
            x_quant_T(1)

    P.ops["sync"].append({"fn": lambda s: None, "waits": [(d_o, 16 * NCH)],
                          "inc": None, "tag": "final"})

    with nc.Block() as block:
        sems = []
        for e in engs:
            g = nc.semaphore(f"sem_{e.name}")
            e.sem = g.__enter__()
            sems.append(g)
        block.tensor(lambda t: P.run_body(t, "tensor"))
        block.vector(lambda v: P.run_body(v, "vector"))
        block.scalar(lambda a: P.run_body(a, "scalar"))
        block.gpsimd(lambda g: P.run_body(g, "gpsimd"))
        block.sync(lambda s: P.run_body(s, "sync"))
    for g in reversed(ctx):
        g.__exit__(None, None, None)
    for g in reversed(sems):
        g.__exit__(None, None, None)
    return nc


_NC_CACHE = {}


def kernel(L0_attn_x, L0_attn_Wq, L0_attn_Wk, L0_attn_Wv, L0_attn_betas, L0_attn_Wo,
           L0_ffn_w_gate, L0_ffn_w_up, L0_ffn_w_down):
    if "nc" not in _NC_CACHE:
        _NC_CACHE["nc"] = build(STAGE)
    nc = _NC_CACHE["nc"]
    x = np.ascontiguousarray(np.asarray(L0_attn_x, dtype=np.float32))
    common = {
        "wq": np.ascontiguousarray(np.asarray(L0_attn_Wq, dtype=np.float32)),
        "wk": np.ascontiguousarray(np.asarray(L0_attn_Wk, dtype=np.float32)),
        "wv": np.ascontiguousarray(np.asarray(L0_attn_Wv, dtype=np.float32)),
        "wo": np.ascontiguousarray(np.asarray(L0_attn_Wo, dtype=np.float32)),
        "betas": np.ascontiguousarray(
            np.asarray(L0_attn_betas, dtype=np.float32).reshape(1, 512)),
        "wg": np.ascontiguousarray(np.asarray(L0_ffn_w_gate, dtype=np.float32)),
        "wu": np.ascontiguousarray(np.asarray(L0_ffn_w_up, dtype=np.float32)),
        "wd": np.ascontiguousarray(np.asarray(L0_ffn_w_down, dtype=np.float32)),
    }
    in_maps = [dict(common, x=np.ascontiguousarray(x[b])) for b in range(8)]
    res = run_bass_kernel_spmd(nc, in_maps, core_ids=list(range(8)))
    return np.stack([r["y"] for r in res.results], axis=0)


# version 17
# speedup vs baseline: 1.0804x; 1.0804x over previous
import numpy as np
import concourse.bass as bass
import concourse.mybir as mybir
from concourse.bass_utils import run_bass_kernel_spmd

F32 = mybir.dt.float32
BF16 = mybir.dt.bfloat16
FP8 = mybir.dt.float8e4
AF = mybir.ActivationFunctionType
ALU = mybir.AluOpType
AX = mybir.AxisListType.X

C23 = 12582912.0
C52 = 6755399441055744.0
EPS = 1e-5
NCH, D, INNER = 16, 1024, 4096
SEQ = NCH * 256
STAGE = 3
DEBUG_TAPS = ()
LAST_COL_IDS = {}


class Res:
    __slots__ = ("name", "writer", "readers")

    def __init__(self, name):
        self.name = name
        self.writer = None
        self.readers = []


class EngS:
    def __init__(self, name, body, step=1):
        self.name, self.body, self.step = name, body, step
        self.n = 0
        self.sem = None
        self.pending = []
        self.last_op = None
        self.max_waited = {}


class Prog:
    def __init__(self):
        self.ops = {k: [] for k in ("tensor", "vector", "scalar", "gpsimd", "sync")}

    def fence(self, e):
        if e.last_op is None:
            assert not e.pending
            return
        e.n += e.step
        e.last_op["inc"] = (e, e.step)
        for kind, r in e.pending:
            if kind == "w":
                r.writer = (e, e.n)
                r.readers = []
            else:
                r.readers.append((e, e.n))
        e.pending = []
        e.last_op = None

    def op(self, e, fn, reads=(), writes=(), tag=""):
        waits = {}

        def need(src):
            if src is None:
                return
            s, v = src
            if s is e or e.max_waited.get(s, -1) >= v:
                return
            waits[s] = max(waits.get(s, 0), v)

        for r in reads:
            need(r.writer)
        for r in writes:
            need(r.writer)
            for rd in r.readers:
                need(rd)
        for s, v in waits.items():
            if v > s.n:
                raise RuntimeError(f"wait on unfenced {s.name}={v} (have {s.n}) at {tag}")
            e.max_waited[s] = v
        rec = {"fn": fn, "waits": list(waits.items()), "inc": None, "tag": tag}
        self.ops[e.body].append(rec)
        e.last_op = rec
        for r in reads:
            e.pending.append(("r", r))
        for r in writes:
            e.pending.append(("w", r))
        return rec

    def handoff(self, dsts, srcs):
        ev = []
        for r in srcs:
            if r.writer is not None:
                ev.append(r.writer)
            ev.extend(r.readers)
        for d in dsts:
            d.writer = None
            d.readers = list(ev)

    def run_body(self, engobj, body):
        for rec in self.ops[body]:
            for s, v in rec["waits"]:
                engobj.wait_ge(s.sem, v)
            instr = rec["fn"](engobj)
            if rec["inc"] is not None:
                s, k = rec["inc"]
                instr.then_inc(s.sem, k)


def build(stage=STAGE):
    nc = bass.Bass("TRN2", target_bir_lowering=False, debug=False)
    P = Prog()
    pe = EngS("pe", "tensor")
    dv = EngS("dv", "vector")
    ac = EngS("ac", "scalar")
    gp = EngS("gp", "gpsimd")
    d_x = EngS("d_x", "sync", 16)
    d_w = EngS("d_w", "gpsimd", 16)
    d_o = EngS("d_o", "sync", 16)
    d_t = EngS("d_t", "sync", 16)
    engs = [pe, dv, ac, gp, d_x, d_w, d_o, d_t]

    x_h = nc.dram_tensor("x", [SEQ, D], F32, kind="ExternalInput").ap()
    wq_h = nc.dram_tensor("wq", [D, 512], F32, kind="ExternalInput").ap()
    wk_h = nc.dram_tensor("wk", [D, 512], F32, kind="ExternalInput").ap()
    wv_h = nc.dram_tensor("wv", [D, 512], F32, kind="ExternalInput").ap()
    wo_h = nc.dram_tensor("wo", [512, D], F32, kind="ExternalInput").ap()
    bet_h = nc.dram_tensor("betas", [1, 512], F32, kind="ExternalInput").ap()
    wg_h = nc.dram_tensor("wg", [INNER, D], F32, kind="ExternalInput").ap()
    wu_h = nc.dram_tensor("wu", [INNER, D], F32, kind="ExternalInput").ap()
    wd_h = nc.dram_tensor("wd", [D, INNER], F32, kind="ExternalInput").ap()
    y_h = nc.dram_tensor("y", [SEQ, D], F32, kind="ExternalOutput").ap()

    ctx = []

    def sb(name, shape, dt):
        g = nc.sbuf_tensor(name, shape, dt)
        t = g.__enter__()
        ctx.append(g)
        return t

    def pst(name, shape, dt):
        g = nc.psum_tensor(name, shape, dt)
        t = g.__enter__()
        ctx.append(g)
        return t

    wgT = sb("wgT", [128, 8 * INNER], FP8)
    wuT = sb("wuT", [128, 8 * INNER], FP8)
    wdT = sb("wdT", [128, 32 * 1024], FP8)
    wqb = sb("wqb", [128, 8 * 512], BF16)
    wkb = sb("wkb", [128, 8 * 512], BF16)
    wvb = sb("wvb", [128, 8 * 512], BF16)
    wob = sb("wob", [128, 4 * 1024], BF16)
    XIN = sb("XIN", [128, 2048], F32)
    X1 = sb("X1", [128, 2048], F32)
    X2 = sb("X2", [128, 2048], F32)
    HT = sb("HT", [128, 4096], F32)
    HQ = sb("HQ", [128, 4096], BF16)
    FMP = sb("FMP", [128, 4096], BF16)
    XQ = sb("XQ", [128, 1024], BF16)
    XQF = sb("XQF", [128, 1024], BF16)
    SCR = sb("SCR", [128, 2048], F32)
    SC2 = sb("SC2", [128, 1024], F32)
    HTb = HT.ap().bitcast(BF16)

    QB = HTb[:, 0:1024]
    KB = HTb[:, 1024:2048]
    SQF = HTb[:, 2048:3072]
    SKF = HTb[:, 3072:4096]
    SKT = HTb[:, 4096:5120]
    VT = HTb[:, 5120:6144]
    VP = HTb[:, 6144:7168]
    ATT = HTb[:, 7168:8192]
    LNB = HQ.ap()[:, 0:2048]
    EB0 = HQ.ap()[:, 2048:3072]
    EB1 = HQ.ap()[:, 0:1024]
    ETB = HQ.ap()[:, 3072:3584]
    ETB1 = HQ.ap()[:, 3584:4096]
    ATF = FMP.ap()[:, 0:1024]
    LNF = FMP.ap()[:, 1024:3072]
    MEMZ = sb("MEMZ", [128, 260], BF16)
    ZS = sb("ZS", [128, 4], F32)
    GB = sb("GB", [128, 256], F32)
    OGF = sb("OGF", [128, 512], BF16)
    COLS = sb("COLS", [128, 176], F32)
    IDN = sb("IDN", [128, 128], BF16)
    ONE1 = sb("ONE1", [1, 128], BF16)
    ONE1F = sb("ONE1F", [1, 128], F32)
    ONEC = sb("ONEC", [128, 1], F32)
    ROWS = SCR.ap()[0:1, 0:1024]
    ROWB = HTb[0:1, 6144:7168]
    STAT = sb("STAT", [128, 16], F32)
    WST = HT.ap()[:, 0:4096]
    WQ3 = HQ.ap()[:, 0:2048]

    pT0 = pst("pT0", [128, 1024], BF16)
    pT1 = pst("pT1", [128, 1024], BF16)
    pW0 = pst("pW0", [128, 512], F32)
    pW1 = pst("pW1", [128, 512], F32)
    pW2 = pst("pW2", [128, 512], F32)
    pD = pst("pD", [128, 1024], F32)
    pM = pst("pM", [128, 512], F32)

    R = Res
    r_wgT, r_wuT, r_wdT = R("wgT"), R("wuT"), R("wdT")
    r_awt = R("attn_w")
    r_xin, r_x1, r_x2 = R("XIN"), R("X1"), R("X2")
    r_ht, r_xq, r_xqf = R("HT"), R("XQ"), R("XQF")
    r_hqq = [R(f"HQ{i}") for i in range(4)]
    r_fmpq = [R(f"FMP{i}") for i in range(4)]
    r_scr, r_sc2 = R("SCR"), R("SC2")
    r_u, r_t1 = R("U"), R("T1")
    r_s2b0, r_s2b1, r_s2b2 = R("s2b0"), R("s2b1"), R("s2b2")
    r_wstA, r_wstB = R("wstA"), R("wstB")
    r_lnb, r_lnf = R("LNB"), R("LNF")
    r_qb, r_kb, r_sqf, r_skf, r_skt = R("QB"), R("KB"), R("SQF"), R("SKF"), R("SKT")
    r_vt, r_vp, r_eb, r_etb = R("VT"), R("VP"), R("EB"), R("ETB")
    r_eb1 = R("EB1")
    r_att, r_atf = R("ATT"), R("ATF")
    r_memz, r_zs, r_gb, r_ogf = R("MEMZ"), R("ZS"), R("GB"), R("OGF")
    r_cols, r_idn, r_ones = R("COLS"), R("IDN"), R("ONES")
    r_rows, r_stat = R("ROWS"), R("STAT")
    r_wst, r_wq3 = R("WST"), R("WQ3")
    r_tb1, r_tb3 = R("TQ1"), R("TQ3")
    r_tcol = R("TCOL")
    r_pT0, r_pT1 = R("pT0"), R("pT1")
    r_pW0, r_pW1, r_pW2, r_pD, r_pM = R("pW0"), R("pW1"), R("pW2"), R("pD"), R("pM")

    r_qb2, r_kb2 = r_qb, r_kb
    FFN_SIDE = [r_ht, r_u, r_t1, r_s2b0, r_s2b1, r_s2b2] + r_hqq + r_fmpq
    ATTN_SIDE = [r_qb, r_kb, r_sqf, r_skf, r_skt, r_vt, r_vp, r_att,
                 r_lnb, r_eb, r_eb1, r_etb, r_atf, r_lnf, r_sc2]

    col_ids = {}
    global LAST_COL_IDS
    LAST_COL_IDS = col_ids

    def col(name):
        if name not in col_ids:
            col_ids[name] = len(col_ids)
            assert len(col_ids) <= 176, "out of COLS"
        i = col_ids[name]
        return COLS[:, i:i + 1]

    def dbg(name, ap, reads):
        if name not in DEBUG_TAPS:
            return
        h = nc.dram_tensor("dbg_" + name, list(ap.shape), ap.dtype,
                           kind="ExternalOutput").ap()
        P.op(d_o, lambda s, h=h, ap=ap: s.dma_start(out=h, in_=ap),
             reads=tuple(reads), tag="dbg_" + name)
        P.fence(d_o)

    def g_setup(g):
        g.memset(IDN[:, :], 0.0)
        g.affine_select(out=IDN[:, :], in_=IDN[:, :], compare_op=ALU.not_equal,
                        fill=1.0, base=0, pattern=[[-1, 128]], channel_multiplier=1)
        g.memset(ONE1[:, :], 1.0)
        g.memset(ONE1F[:, :], 1.0)
        g.memset(ONEC[:, :], 1.0)
        g.memset(ZS[:, :], 1.0 / 64.0)
        return g.memset(MEMZ[:, :], 0.0)
    P.op(gp, g_setup, writes=(r_idn, r_ones, r_zs, r_memz), tag="setup")
    P.fence(gp)

    def d_flag(v):
        v.memset(col("k75"), 0.75)
        v.drain()
        v.tensor_scalar(col("flag"), col("k75"), C52, None, op0=ALU.add)
        v.drain()
        v.tensor_scalar(col("flag"), col("flag"), C52, None, op0=ALU.subtract)
        v.drain()
        return v.tensor_scalar(col("om"), col("flag"), -1.0, 1.0, op0=ALU.mult, op1=ALU.add)
    P.op(dv, d_flag, writes=(r_cols,), tag="flag")
    P.fence(dv)
    dbg("cols_flag", COLS[:, :], [r_cols])

    for (h, t, kk) in ((wq_h, wqb, 8), (wk_h, wkb, 8), (wv_h, wvb, 8), (wo_h, wob, 4)):
        def ld(g, h=h, t=t, kk=kk):
            return g.dma_start(out=t.ap().rearrange("p (k n) -> p k n", k=kk),
                               in_=h.rearrange("(k p) n -> p k n", p=128))
        P.op(d_w, ld, writes=(r_awt,), tag="wload")
        P.fence(d_w)
    P.op(d_w, lambda g: g.dma_start(out=ROWS[0:1, 0:512], in_=bet_h),
         writes=(r_rows, r_sc2), tag="betas")
    P.fence(d_w)

    def x_load(c):
        def ld(s, c=c):
            return s.dma_start(
                out=XIN.ap().rearrange("p (t n) -> p t n", t=2),
                in_=x_h[c * 256:(c + 1) * 256, :].rearrange("(t p) n -> p t n", p=128))
        P.op(d_x, ld, writes=(r_xin,), tag=f"xload{c}")
        P.fence(d_x)

    x_load(0)

    P.op(ac, lambda a: a.activation(ROWS[0:1, 512:1024], ROWS[0:1, 0:512], AF.Sigmoid),
         reads=(r_rows,), writes=(r_rows,), tag="sig")
    P.fence(ac)

    def d_ogrow(v):
        v.tensor_scalar(ROWS[0:1, 0:512], ROWS[0:1, 512:1024], -1.0, 1.0,
                        op0=ALU.mult, op1=ALU.add)
        v.tensor_copy(ROWB[0:1, 0:512], ROWS[0:1, 0:512])
        return v.tensor_copy(ROWB[0:1, 512:1024], ROWS[0:1, 512:1024])
    P.op(dv, d_ogrow, reads=(r_rows, r_sc2), writes=(r_rows, r_ht), tag="ogrow")
    P.fence(dv)
    dbg("rowb", ROWB[0:1, 0:1024], [r_rows])
    dbg("rows", ROWS[0:1, 0:1024], [r_rows])

    P.op(pe, lambda t: t.matmul(pW0[:, 0:512], ONE1[0:1, :], ROWB[0:1, 0:512],
                                start=True, stop=True),
         reads=(r_rows, r_ones), writes=(r_pW0,), tag="ogbc")
    P.fence(pe)
    P.op(dv, lambda v: v.tensor_copy(OGF[:, :], pW0[:, 0:512]),
         reads=(r_pW0,), writes=(r_ogf,), tag="ogf")
    P.fence(dv)
    dbg("ogf", OGF[:, :], [r_ogf])
    dbg("gbf", GB[:, :], [r_gb])
    for m in range(4):
        for par in range(2):
            def gbc(t, m=m, par=par):
                return t.matmul(pW1[par * 64:(par + 1) * 64, m * 64:(m + 1) * 64],
                                ONE1[0:1, 0:64],
                                ROWB[0:1, 512 + (2 * m + par) * 64:512 + (2 * m + par + 1) * 64],
                                start=True, stop=True,
                                tile_position=(0, par * 64))
            P.op(pe, gbc, reads=(r_rows, r_ones), writes=(r_pW1,), tag="gbc")
    P.fence(pe)
    P.op(dv, lambda v: v.tensor_copy(GB[:, :], pW1[:, 0:256]),
         reads=(r_pW1,), writes=(r_gb,), tag="gbf")
    P.fence(dv)

    def tern_A_tile(wname, src_fn, i):
        acc, tmp = col("acc_" + wname), col("tmp_" + wname)
        off = (i % 2) * 1024
        def ld(sq, i=i, off=off):
            return sq.dma_start(out=WST[:, off:off + 1024], in_=src_fn(i))
        P.op(d_x, ld, writes=(r_wstA,), tag="tldA")
        P.fence(d_x)
        def a_abs(a, off=off):
            return a.activation(SCR[:, 1024:2048], WST[:, off:off + 1024],
                                AF.Abs, accum_out=tmp)
        P.op(ac, a_abs, reads=(r_wstA,), writes=(r_t1, r_tcol), tag="abs")
        P.fence(ac)
        if i == 0:
            P.op(dv, lambda v: v.tensor_copy(acc, tmp),
                 reads=(r_tcol,), writes=(r_tcol,), tag="acc")
        else:
            def d_acc(v):
                v.drain()
                return v.tensor_add(acc, acc, tmp)
            P.op(dv, d_acc, reads=(r_tcol,), writes=(r_tcol,), tag="acc")
        P.fence(dv)

    def tern_scale(wname):
        acc = col("acc_" + wname)
        P.op(pe, lambda t: t.matmul(pW2[0:1, 0:1], acc, ONEC[:, :],
                                    start=True, stop=True),
             reads=(r_tcol, r_ones), writes=(r_pW2,), tag="tsum")
        P.fence(pe)
        def d_scale(v):
            v.tensor_scalar(STAT[0:1, 14:15], pW2[0:1, 0:1],
                            1.0 / (INNER * D), 1e-5, op0=ALU.mult, op1=ALU.max)
            v.drain()
            return v.reciprocal(STAT[0:1, 15:16], STAT[0:1, 14:15])
        P.op(dv, d_scale, reads=(r_pW2,), writes=(r_stat,), tag="scale")
        P.fence(dv)
        P.op(pe, lambda t: t.matmul(pW2[:, 4:6], ONE1F[0:1, :],
                                    STAT[0:1, 14:16], start=True, stop=True),
             reads=(r_stat, r_ones), writes=(r_pW2,), tag="scbc")
        P.fence(pe)
        def d_cols(v, wname=wname):
            v.tensor_copy(col("swrec_" + wname), pW2[:, 4:5])
            v.tensor_copy(col("wsc_" + wname), pW2[:, 5:6])
            v.drain()
            return v.tensor_mul(col("wsck_" + wname), col("wsc_" + wname),
                                col("flag"))
        P.op(dv, d_cols, reads=(r_pW2, r_cols), writes=(r_cols,), tag="wcols")
        P.fence(dv)

    def tern_B_parity(i):
        if i % 2 == 0:
            q1, q3 = SC2[:, 0:1024], SCR[:, 0:1024]
            rq1, rq3 = (r_s2b0, r_s2b1, r_sc2), (r_u,)
        else:
            q1, q3 = X1[:, 0:1024], X1[:, 1024:2048]
            rq1, rq3 = (r_tb1,), (r_tb3,)
        pb, rpb = (pT0, r_pT0) if i % 2 == 0 else (pT1, r_pT1)
        return q1, q3, rq1, rq3, pb, rpb

    def tern_B_front(wname, src_fn, i):
        wsc, wsck = col("wsc_" + wname), col("wsck_" + wname)
        off = (2 + i % 2) * 1024
        q1, q3, rq1, rq3, pb, rpb = tern_B_parity(i)
        def ld(g, i=i, off=off):
            return g.dma_start(out=WST[:, off:off + 1024], in_=src_fn(i))
        P.op(d_w, ld, writes=(r_wstB,), tag="tldB")
        P.fence(d_w)
        def a_t1(a, off=off, wsc=wsc, q1=q1):
            return a.activation(q1, WST[:, off:off + 1024],
                                AF.Copy, bias=C23, scale=wsc)
        P.op(ac, a_t1, reads=(r_wstB, r_cols), writes=rq1, tag="t1")
        P.fence(ac)
        def g_t3(g, off=off, wsck=wsck, q3=q3):
            return g.tensor_scalar(q3, WST[:, off:off + 1024],
                                   wsck, C52, op0=ALU.mult, op1=ALU.add)
        P.op(gp, g_t3, reads=(r_wstB, r_cols), writes=rq3, tag="t3")
        P.fence(gp)
        def d_q(v, i=i, q1=q1, q3=q3):
            v.tensor_scalar(q1, q1, C23, col("om"),
                            op0=ALU.subtract, op1=ALU.mult)
            v.scalar_tensor_tensor(q1, q3, C52, q1,
                                   op0=ALU.subtract, op1=ALU.add)
            wslot = (i % 2) * 1024
            return v.tensor_scalar(WQ3[:, wslot:wslot + 1024],
                                   q1, 1.0, -1.0, op0=ALU.min, op1=ALU.max)
        P.op(dv, d_q, reads=rq1 + rq3 + (r_cols,),
             writes=rq1 + rq3 + (r_wq3,), tag="q3")
        P.fence(dv)

    def tern_B_back(r_dst, dst_fn, i):
        q1, q3, rq1, rq3, pb, rpb = tern_B_parity(i)
        wslot = (i % 2) * 1024
        for b in range(8):
            def tr(t, b=b, wslot=wslot, pb=pb):
                return t.transpose(pb[:, b * 128:(b + 1) * 128],
                                   WQ3[:, wslot + b * 128:wslot + (b + 1) * 128],
                                   IDN[:, :])
            P.op(pe, tr, reads=(r_wq3, r_idn), writes=(rpb,), tag="wtr")
        P.fence(pe)
        def a_ev(a, i=i, pb=pb):
            return a.copy(dst_fn(i), pb[:, :].rearrange("p (k n) -> p k n", k=8))
        P.op(ac, a_ev, reads=(rpb,), writes=(r_dst,), tag="wev")
        P.fence(ac)

    TERN = [
        ("g", lambda i: wg_h[i * 128:(i + 1) * 128, :], r_wgT,
         lambda i: wgT.ap().rearrange("p (k n) -> p k n", k=8)[:, :, i * 128:(i + 1) * 128]),
        ("u", lambda i: wu_h[i * 128:(i + 1) * 128, :], r_wuT,
         lambda i: wuT.ap().rearrange("p (k n) -> p k n", k=8)[:, :, i * 128:(i + 1) * 128]),
        ("d", lambda i: wd_h[(i // 4) * 128:(i // 4 + 1) * 128,
                             (i % 4) * 1024:(i % 4 + 1) * 1024], r_wdT,
         lambda i: wdT.ap().rearrange("p (k n) -> p k n", k=32)
         [:, (i % 4) * 8:(i % 4) * 8 + 8, (i // 4) * 128:(i // 4 + 1) * 128]),
    ]
    for i in range(32):
        tern_A_tile(TERN[0][0], TERN[0][1], i)
    tern_scale(TERN[0][0])
    for k in range(3):
        wname, src_fn, r_dst, dst_fn = TERN[k]
        for i in range(32):
            tern_B_front(wname, src_fn, i)
            if k + 1 < 3:
                tern_A_tile(TERN[k + 1][0], TERN[k + 1][1], i)
            if i > 0:
                tern_B_back(r_dst, dst_fn, i - 1)
        tern_B_back(r_dst, dst_fn, 31)
        if k + 1 < 3:
            tern_scale(TERN[k + 1][0])
    P.handoff([r_x1], [r_tb1, r_tb3])

    dbg("wgt", wgT[:, 0:4096], [r_wgT])
    dbg("wdt", wdT[:, 0:4096], [r_wdT])

    def ln_stats(src_res, src_ap, pref):
        mcol, rcol = col("m_" + pref), col("r_" + pref)
        def d_st(v, src_ap=src_ap):
            v.bn_stats(STAT[:, 0:6], src_ap[:, 0:512])
            v.bn_stats(STAT[:, 6:12], src_ap[:, 512:1024])
            v.drain()
            v.bn_aggr(STAT[:, 12:14], STAT[:, 0:12])
            v.drain()
            v.tensor_copy(mcol, STAT[:, 12:13])
            return v.tensor_scalar(col("tv"), STAT[:, 13:14], EPS, None, op0=ALU.add)
        P.op(dv, d_st, reads=(src_res,), writes=(r_stat, r_cols), tag="lnst_" + pref)
        P.fence(dv)
        def a_r(a):
            a.activation(col("tv2"), col("tv"), AF.Ln)
            a.drain()
            return a.activation(rcol, col("tv2"), AF.Exp, scale=-0.5)
        P.op(ac, a_r, reads=(r_cols,), writes=(r_cols,), tag="lnr_" + pref)
        P.fence(ac)
        return mcol, rcol

    _trot = [0]

    def transpose_group(src_ap_fn, n, pbank_ignored, r_ignored, dst_ap, r_src, r_dst,
                        tag):
        rot = _trot[0]
        _trot[0] += 1
        pbank, r_pbank = (pT0, r_pT0) if rot % 2 == 0 else (pT1, r_pT1)
        for b in range(n):
            def tr(t, b=b, pbank=pbank):
                return t.transpose(pbank[:, b * 128:(b + 1) * 128], src_ap_fn(b), IDN[:, :])
            P.op(pe, tr, reads=(r_src, r_idn), writes=(r_pbank,), tag="tr_" + tag)
        P.fence(pe)
        eng = dv if (rot // 2) % 2 == 0 else ac
        def ev(e, pbank=pbank):
            return e.tensor_copy(dst_ap,
                                 pbank[:, 0:n * 128].rearrange("p (k n) -> p k n", k=n))
        if eng is dv:
            P.op(dv, ev, reads=(r_pbank,), writes=(r_dst,), tag="ev_" + tag)
            P.fence(dv)
        else:
            P.op(ac, lambda a, pbank=pbank: a.copy(
                dst_ap, pbank[:, 0:n * 128].rearrange("p (k n) -> p k n", k=n)),
                reads=(r_pbank,), writes=(r_dst,), tag="ev_" + tag)
            P.fence(ac)

    def xbar_T(in_ap, out_ap, r_src, r_dst, tag):
        def tr(sq):
            return sq.dma_start(out=out_ap, in_=in_ap, transpose=True)
        P.op(d_t, tr, reads=(r_src,), writes=(r_dst,), tag="xT_" + tag)
        P.fence(d_t)

    U_AP = SCR[:, 0:1024]
    T1_AP = SCR[:, 1024:2048]
    SILU_AP = (SC2[:, 0:512], SC2[:, 512:1024])
    SILU_RES = (r_s2b0, r_s2b1)

    def x_quant(pref, ap_, par, tau, in_buf, r_in, t3ap, r_t3):
        mcol = col(f"m_{pref}{tau}_{par}")
        rcol = col(f"r_{pref}{tau}_{par}")
        def d_u(v, tau=tau, mcol=mcol, ap_=ap_):
            v.tensor_scalar(U_AP, in_buf[:, tau * 1024:(tau + 1) * 1024], mcol,
                            None, op0=ALU.subtract)
            return v.reduce_max(col(f"mxm{tau}_{ap_}"), U_AP, axis=AX,
                                apply_absolute_value=True)
        P.op(dv, d_u, reads=(r_in, r_cols), writes=(r_u, r_cols), tag="u")
        def d_sc(v, tau=tau, rcol=rcol, ap_=ap_):
            q = f"{tau}_{ap_}"
            v.drain()
            v.tensor_scalar(col(f"mcl{q}"), col(f"mxm{q}"), 1e-5, None, op0=ALU.max)
            v.tensor_mul(col(f"rmx{q}"), rcol, col(f"mxm{q}"))
            v.drain()
            v.reciprocal(col(f"srec{q}"), col(f"mcl{q}"))
            v.tensor_scalar(col(f"rmx{q}"), col(f"rmx{q}"), 1e-5, None, op0=ALU.max)
            v.drain()
            v.tensor_scalar(col(f"s{q}"), col(f"srec{q}"), 127.0, None, op0=ALU.mult)
            v.tensor_mul(col(f"cg{q}"), col(f"rmx{q}"), col("swrec_g"))
            v.tensor_mul(col(f"cu{q}"), col(f"rmx{q}"), col("swrec_u"))
            v.drain()
            v.tensor_mul(col(f"skm{q}"), col(f"s{q}"), col("flag"))
            v.tensor_scalar(col(f"cg{q}"), col(f"cg{q}"), 1.0 / 127.0, None,
                            op0=ALU.mult)
            return v.tensor_scalar(col(f"cu{q}"), col(f"cu{q}"), 1.0 / 127.0,
                                   None, op0=ALU.mult)
        P.op(dv, d_sc, reads=(r_cols,), writes=(r_cols,), tag="sc")
        P.fence(dv)
        def a_t1(a, tau=tau, ap_=ap_):
            return a.activation(T1_AP, U_AP, AF.Copy, bias=C23,
                                scale=col(f"s{tau}_{ap_}"))
        P.op(ac, a_t1, reads=(r_u, r_cols), writes=(r_t1,), tag="xt1")
        P.fence(ac)
        def g_t3(g, tau=tau, ap_=ap_, t3ap=t3ap):
            return g.tensor_scalar(t3ap, U_AP, col(f"skm{tau}_{ap_}"), C52,
                                   op0=ALU.mult, op1=ALU.add)
        P.op(gp, g_t3, reads=(r_u, r_cols), writes=(r_t3,), tag="xt3")
        P.fence(gp)
        def d_x(v, t3ap=t3ap):
            v.tensor_scalar(T1_AP, T1_AP, C23, col("om"), op0=ALU.subtract,
                            op1=ALU.mult)
            return v.scalar_tensor_tensor(XQ[:, :], t3ap, C52, T1_AP,
                                          op0=ALU.subtract, op1=ALU.add)
        P.op(dv, d_x, reads=(r_t1, r_t3, r_cols), writes=(r_t1, r_xq), tag="xq")
        P.fence(dv)

    def x_quant_T(tau):
        if tau == 0:
            dst = XQF.ap().rearrange("p (k n) -> p k n", k=8)
            rdst = r_xqf
        else:
            dst = FMP[:, 0:1024].rearrange("p (k n) -> p k n", k=8)
            rdst = r_fmpq[0]
        xbar_T(XQ.ap().rearrange("p (k n) -> p k n", k=8), dst, r_xq, rdst, "xq")

    FFN_COL_NAMES = ("mxm", "mcl", "srec", "s", "skm", "rmx", "cg", "cu",
                     "mxh", "t2c", "s2r", "s2", "c2", "sig", "sigk")

    def reg_ffn_cols(ap_):
        for t_ in range(2):
            for n_ in FFN_COL_NAMES:
                col(f"{n_}{t_}_{ap_}")
            for sl_ in range(8):
                col(f"hmx{sl_}_{t_}_{ap_}")

    def ffn(c, in_buf, r_in, out_buf, r_out, scr_buf, r_scrb, pref,
            xpath_done=False, tail_a=None, tail_b=None, tail_gu=None):
        ap_ = 0 if pref == "f1" else 1
        reg_ffn_cols(ap_)
        dsts = FFN_SIDE if not xpath_done else \
            [r for r in FFN_SIDE if r not in tuple(r_fmpq) + (r_t1, r_u)]
        P.handoff(dsts, ATTN_SIDE + [r_wst, r_wstA, r_wstB, r_wq3, r_rows, r_scr])
        par = c % 2

        def emit_xpath(tau):
            x_quant(pref, ap_, par, tau, in_buf, r_in,
                    scr_buf[:, tau * 1024:(tau + 1) * 1024], r_scrb)

        def emit_xqT(tau):
            x_quant_T(tau)

        def lhsT_ap(tau, K):
            src = XQF if tau == 0 else FMP
            return src[:, K * 128:(K + 1) * 128]

        GU_BANKS = [(pW0, r_pW0), (pW1, r_pW1), (pW2, r_pW2)]

        def emit_gu_slice(tau, sl, nb):
            banks = GU_BANKS
            rsrc = r_xqf if tau == 0 else r_fmpq[0]
            if True:
                bg, rbg = banks[nb % 3]; nb += 1
                bu, rbu = banks[nb % 3]; nb += 1
                for K in range(8):
                    def mmg(t, K=K, sl=sl, tau=tau, bg=bg):
                        return t.matmul(bg[:, :], lhsT_ap(tau, K),
                                        wgT[:, K * INNER + sl * 512:K * INNER + (sl + 1) * 512],
                                        start=(K == 0), stop=(K == 7))
                    P.op(pe, mmg, reads=(rsrc, r_wgT), writes=(rbg,), tag="mmg")
                P.fence(pe)
                for K in range(8):
                    def mmu(t, K=K, sl=sl, tau=tau, bu=bu):
                        return t.matmul(bu[:, :], lhsT_ap(tau, K),
                                        wuT[:, K * INNER + sl * 512:K * INNER + (sl + 1) * 512],
                                        start=(K == 0), stop=(K == 7))
                    P.op(pe, mmu, reads=(rsrc, r_wuT), writes=(rbu,), tag="mmu")
                P.fence(pe)
                sap, sres = SILU_AP[sl % 2], SILU_RES[sl % 2]
                def a_silu(a, tau=tau, bg=bg, sap=sap, ap_=ap_):
                    return a.activation(sap, bg[:, :], AF.Silu,
                                        scale=col(f"cg{tau}_{ap_}"))
                P.op(ac, a_silu, reads=(rbg, r_cols), writes=(sres,), tag="silu")
                P.fence(ac)
                def d_h(v, sl=sl, tau=tau, bu=bu, sap=sap, ap_=ap_):
                    v.tensor_mul(HT[:, sl * 512:(sl + 1) * 512], sap, bu[:, :])
                    return v.reduce_max(col(f"hmx{sl}_{tau}_{ap_}"),
                                        HT[:, sl * 512:(sl + 1) * 512], axis=AX,
                                        apply_absolute_value=True)
                P.op(dv, d_h, reads=(sres, rbu), writes=(r_ht, r_cols), tag="h")
                P.fence(dv)
            return nb

        def emit_s2chain(tau):
            q = f"{tau}_{ap_}"
            c0 = col_ids[f"hmx0_{q}"]
            assert col_ids[f"hmx7_{q}"] == c0 + 7
            def d_s2(v, q=q, c0=c0):
                v.drain()
                v.reduce_max(col(f"mxh{q}"), COLS[:, c0:c0 + 8], axis=AX)
                v.drain()
                v.tensor_mul(col(f"t2c{q}"), col(f"cu{q}"), col(f"mxh{q}"))
                v.drain()
                v.tensor_scalar(col(f"t2c{q}"), col(f"t2c{q}"), 1e-5, None,
                                op0=ALU.max)
                v.drain()
                v.reciprocal(col(f"s2r{q}"), col(f"t2c{q}"))
                v.tensor_mul(col(f"c2{q}"), col(f"t2c{q}"), col("swrec_d"))
                v.drain()
                v.tensor_scalar(col(f"s2{q}"), col(f"s2r{q}"), 127.0, None,
                                op0=ALU.mult)
                v.tensor_scalar(col(f"c2{q}"), col(f"c2{q}"), 0.5 / 127.0, None,
                                op0=ALU.mult)
                v.drain()
                v.tensor_mul(col(f"sig{q}"), col(f"s2{q}"), col(f"cu{q}"))
                v.drain()
                return v.tensor_mul(col(f"sigk{q}"), col(f"sig{q}"), col("flag"))
            P.op(dv, d_s2, reads=(r_cols,), writes=(r_cols,), tag="s2")
            P.fence(dv)

        def emit_hqq_T(tau, q):
            xbar_T(HQ[:, q * 1024:(q + 1) * 1024].rearrange("p (k n) -> p k n", k=8),
                   FMP[:, q * 1024:(q + 1) * 1024].rearrange("p (k n) -> p k n", k=8),
                   r_hqq[q], r_fmpq[q], f"hq{q}")

        def emit_round_quarter(tau, q, defer_dma=False):
            t3reg = scr_buf[:, 1024:2048]
            if True:
                hsl = HT[:, q * 1024:(q + 1) * 1024]
                def a_h1(a, tau=tau, hsl=hsl, ap_=ap_):
                    return a.activation(T1_AP, hsl, AF.Copy, bias=C23,
                                        scale=col(f"sig{tau}_{ap_}"))
                P.op(ac, a_h1, reads=(r_ht, r_cols), writes=(r_t1,), tag="h1")
                P.fence(ac)
                def g_h3(g, tau=tau, hsl=hsl, t3reg=t3reg, ap_=ap_):
                    return g.tensor_scalar(t3reg, hsl, col(f"sigk{tau}_{ap_}"), C52,
                                           op0=ALU.mult, op1=ALU.add)
                P.op(gp, g_h3, reads=(r_ht, r_cols), writes=(r_scrb,), tag="h3")
                P.fence(gp)
                def d_hq(v, q=q, t3reg=t3reg):
                    v.tensor_scalar(T1_AP, T1_AP, C23, col("om"), op0=ALU.subtract,
                                    op1=ALU.mult)
                    return v.scalar_tensor_tensor(HQ[:, q * 1024:(q + 1) * 1024],
                                                  t3reg, C52, T1_AP,
                                                  op0=ALU.subtract, op1=ALU.add)
                P.op(dv, d_hq, reads=(r_t1, r_scrb, r_cols),
                     writes=(r_t1, r_hqq[q]), tag="hq")
                P.fence(dv)
            if not defer_dma:
                emit_hqq_T(tau, q)

        DOWN_KORDER = list(range(8, 32)) + list(range(8))

        def emit_down(tau):
            for idx, K in enumerate(DOWN_KORDER):
                for half in range(2):
                    def mmd(t, K=K, half=half, idx=idx):
                        return t.matmul(pD[:, half * 512:(half + 1) * 512],
                                        FMP[:, K * 128:(K + 1) * 128],
                                        wdT[:, K * 1024 + half * 512:K * 1024 + (half + 1) * 512],
                                        start=(idx == 0), stop=(idx == 31))
                    P.op(pe, mmd, reads=(r_fmpq[K // 8], r_wdT), writes=(r_pD,), tag="mmd")
            P.fence(pe)

        def emit_downevac(tau):
            to = tau * 1024
            for half in range(2):
                def d_out(v, to=to, half=half, tau=tau, ap_=ap_):
                    return v.scalar_tensor_tensor(
                        out_buf[:, to + half * 512:to + (half + 1) * 512],
                        pD[:, half * 512:(half + 1) * 512], col(f"c2{tau}_{ap_}"),
                        in_buf[:, to + half * 512:to + (half + 1) * 512],
                        op0=ALU.mult, op1=ALU.add)
                P.op(dv, d_out, reads=(r_pD, r_in, r_cols), writes=(r_out,), tag="dout")
                P.fence(dv)

        if not xpath_done:
            emit_xpath(0)
            emit_xqT(0)
            emit_xpath(1)
        nb = 0
        for sl in range(8):
            nb = emit_gu_slice(0, sl, nb)
        if not xpath_done:
            emit_xqT(1)
        emit_s2chain(0)
        nb = 0
        for sl in range(8):
            if sl % 2 == 0:
                emit_round_quarter(0, sl // 2, defer_dma=(sl == 0))
            nb = emit_gu_slice(1, sl, nb)
        if tail_gu is not None:
            tail_gu()
        emit_hqq_T(0, 0)
        emit_s2chain(1)
        emit_down(0)
        for q in range(4):
            emit_round_quarter(1, q)
        if tail_a is not None:
            tail_a()
        emit_downevac(0)
        emit_down(1)
        if tail_b is not None:
            tail_b()
        emit_downevac(1)

    def attn_ln(c, tau):
        mcol, rcol = ln_stats(r_x1, X1[:, tau * 1024:(tau + 1) * 1024], f"at{tau}")
        def d_ln(v, tau=tau, mcol=mcol, rcol=rcol):
            return v.tensor_scalar(LNB[:, tau * 1024:(tau + 1) * 1024],
                                   X1[:, tau * 1024:(tau + 1) * 1024],
                                   mcol, rcol, op0=ALU.subtract, op1=ALU.mult)
        P.op(dv, d_ln, reads=(r_x1, r_cols), writes=(r_lnb, r_hqq[tau]), tag="lnA")
        P.fence(dv)

    def attn(c, ln_done=False):
        dsts = ATTN_SIDE if not ln_done else [r for r in ATTN_SIDE if r is not r_lnb]
        P.handoff(dsts, FFN_SIDE + [r_rows, r_xq, r_wq3])
        par = c % 2
        def d_zm(v):
            for m_ in range(4):
                v.tensor_copy(MEMZ[:, m_ * 65 + 64:m_ * 65 + 65], ZS[:, m_:m_ + 1])
            return v.nop()
        P.op(dv, d_zm, reads=(r_zs,), writes=(r_memz,), tag="zmem")
        P.fence(dv)
        for tau in range(2):
            if not ln_done:
                attn_ln(c, tau)
            xbar_T(LNB[:, tau * 1024:(tau + 1) * 1024].rearrange("p (k n) -> p k n", k=8),
                   LNF.rearrange("p (k n) -> p k n", k=8)[:, :, tau * 128:(tau + 1) * 128],
                   r_lnb, r_lnf, f"lnf{tau}")
        for m in range(4):
            bank, rb = (pW0, r_pW0) if m < 2 else (pW1, r_pW1)
            boff = (m % 2) * 256
            for K in range(8):
                def mmq(t, m=m, K=K, bank=bank, boff=boff):
                    return t.matmul(bank[:, boff:boff + 256],
                                    wqb[:, K * 512 + m * 128:K * 512 + (m + 1) * 128],
                                    LNF[:, K * 256:(K + 1) * 256],
                                    start=(K == 0), stop=(K == 7))
                P.op(pe, mmq, reads=(r_awt, r_lnf), writes=(rb,), tag="mmq")
        P.fence(pe)
        for m in range(4):
            for K in range(8):
                def mmk(t, m=m, K=K):
                    return t.matmul(pD[:, m * 256:(m + 1) * 256],
                                    wkb[:, K * 512 + m * 128:K * 512 + (m + 1) * 128],
                                    LNF[:, K * 256:(K + 1) * 256],
                                    start=(K == 0), stop=(K == 7))
                P.op(pe, mmk, reads=(r_awt, r_lnf), writes=(r_pD,), tag="mmk")
        P.fence(pe)
        for m in range(4):
            bank = pW0 if m < 2 else pW1
            rb = r_pW0 if m < 2 else r_pW1
            boff = (m % 2) * 256
            def a_eq(a, m=m, bank=bank, boff=boff):
                return a.activation(SC2[:, m * 256:(m + 1) * 256], bank[:, boff:boff + 256],
                                    AF.Exp)
            P.op(ac, a_eq, reads=(rb,), writes=(r_sc2,), tag="eq")
            P.fence(ac)
            def a_qc(a, m=m, bank=bank, boff=boff):
                return a.activation(QB[:, m * 256:(m + 1) * 256], bank[:, boff:boff + 256],
                                    AF.Copy, scale=0.125)
            P.op(ac, a_qc, reads=(rb,), writes=(r_qb,), tag="qc")
            P.fence(ac)
            def d_q(v, m=m, bank=bank, boff=boff):
                v.tensor_scalar(SC2[:, m * 256:(m + 1) * 256], SC2[:, m * 256:(m + 1) * 256],
                                1.0, None, op0=ALU.min)
                return v.scalar_tensor_tensor(SQF[:, m * 256:(m + 1) * 256],
                                              bank[:, boff:boff + 256], 0.0,
                                              SC2[:, m * 256:(m + 1) * 256],
                                              op0=ALU.max, op1=ALU.add)
            P.op(dv, d_q, reads=(rb, r_sc2), writes=(r_sqf, r_sc2), tag="dq")
            P.fence(dv)
        if c == 0:
            dbg("lnb", LNB, [r_lnb])
            dbg("qb", QB, [r_qb])
            dbg("sqf", SQF, [r_sqf])
        for m in range(4):
            def a_ek(a, m=m):
                return a.activation(SC2[:, m * 256:(m + 1) * 256], pD[:, m * 256:(m + 1) * 256],
                                    AF.Exp)
            P.op(ac, a_ek, reads=(r_pD,), writes=(r_sc2,), tag="ek")
            P.fence(ac)
            def a_kc(a, m=m):
                return a.activation(KB[:, m * 256:(m + 1) * 256],
                                    pD[:, m * 256:(m + 1) * 256], AF.Copy)
            P.op(ac, a_kc, reads=(r_pD,), writes=(r_kb,), tag="kc")
            P.fence(ac)
            def d_k(v, m=m):
                v.tensor_scalar(SC2[:, m * 256:(m + 1) * 256], SC2[:, m * 256:(m + 1) * 256],
                                1.0, None, op0=ALU.min)
                v.scalar_tensor_tensor(SKF[:, m * 256:(m + 1) * 256],
                                       pD[:, m * 256:(m + 1) * 256], 0.0,
                                       SC2[:, m * 256:(m + 1) * 256],
                                       op0=ALU.max, op1=ALU.add,
                                       accum_out=col(f"zd{m}"))
                v.drain()
                return v.tensor_add(ZS[:, m:m + 1], ZS[:, m:m + 1], col(f"zd{m}"))
            P.op(dv, d_k, reads=(r_pD, r_sc2), writes=(r_skf, r_sc2, r_zs, r_cols),
                 tag="dk")
            P.fence(dv)
        if c == 0:
            dbg("kb", KB, [r_kb])
            dbg("skf", SKF, [r_skf])
            dbg("zs", ZS[:, :], [r_zs])
        for tau in range(2):
            bank, rb = (pW0, r_pW0) if tau == 0 else (pW1, r_pW1)
            for K in range(8):
                def mmv(t, tau=tau, K=K, bank=bank):
                    return t.matmul(bank[:, :],
                                    LNF[:, K * 256 + tau * 128:K * 256 + (tau + 1) * 128],
                                    wvb[:, K * 512:(K + 1) * 512],
                                    start=(K == 0), stop=(K == 7))
                P.op(pe, mmv, reads=(r_awt, r_lnf), writes=(rb,), tag="mmv")
            P.fence(pe)
            def a_vc(a, tau=tau, bank=bank):
                return a.activation(VT[:, tau * 512:(tau + 1) * 512], bank[:, :], AF.Copy)
            P.op(ac, a_vc, reads=(rb,), writes=(r_vt,), tag="vc")
            P.fence(ac)
            def d_v(v, tau=tau, bank=bank):
                return v.tensor_mul(VP[:, tau * 512:(tau + 1) * 512], bank[:, :], OGF[:, :])
            P.op(dv, d_v, reads=(rb, r_ogf), writes=(r_vp,), tag="dv")
            P.fence(dv)
        if c == 0:
            dbg("vt", VT, [r_vt])
            dbg("vp", VP, [r_vp])
        for m in range(4):
            pb, rpb = (pT0, r_pT0) if m < 2 else (pT1, r_pT1)
            for b in range(2):
                def tr(t, m=m, b=b, pb=pb):
                    return t.transpose(pb[:, (2 * (m % 2) + b) * 128:(2 * (m % 2) + b + 1) * 128],
                                       SKF[:, m * 256 + b * 128:m * 256 + (b + 1) * 128],
                                       IDN[:, :])
                P.op(pe, tr, reads=(r_skf, r_idn), writes=(rpb,), tag="sktr")
            if m % 2 == 1:
                P.fence(pe)
                def ev(v, m=m, pb=pb):
                    return v.tensor_copy(
                        SKT.rearrange("p (t q n) -> p t q n", t=2, q=4)
                        [:, :, m - 1:m + 1, :],
                        pb[:, 0:512].rearrange("p (q t n) -> p t q n", q=2, t=2))
                P.op(dv, ev, reads=(rpb,), writes=(r_skt,), tag="skev")
                P.fence(dv)
        for m in range(4):
            for tau in range(2):
                def mmm(t, m=m, tau=tau, c=c):
                    return t.matmul(pM[:, m * 128:(m + 1) * 128],
                                    SKT[:, tau * 512 + m * 128:tau * 512 + (m + 1) * 128],
                                    VT[:, tau * 512 + m * 128:tau * 512 + (m + 1) * 128],
                                    start=(c == 0 and m == 0 and tau == 0), stop=False)
                P.op(pe, mmm, reads=(r_skt, r_vt), writes=(r_pM,), tag="mmm")
        P.fence(pe)
        for hh_ in range(2):
            for i_ in range(2):
                for n_ in ("rd", "rs"):
                    col(f"{n_}{hh_}{i_}")
                for p_ in range(2):
                    for n_ in ("sb", "se"):
                        col(f"{n_}{hh_}{i_}_{p_}")

        def eb_of(m):
            return (EB0, r_eb) if m % 2 == 0 else (EB1, r_eb1)

        def scores_banks(m):
            if m % 2 == 0:
                return ((pW0, r_pW0), (pW1, r_pW1))
            return ((pD[:, 0:512], r_pD), (pD[:, 512:1024], r_pD))

        def emit_scores(m):
            for hh in range(2):
                bank, rb = scores_banks(m)[hh]
                for i in range(2):
                    def mms(t, m=m, hh=hh, i=i, bank=bank):
                        return t.matmul(bank[:, i * 256:(i + 1) * 256],
                                        QB[:, m * 256 + i * 128:m * 256 + (i + 1) * 128]
                                        [hh * 64:(hh + 1) * 64, :],
                                        KB[:, m * 256:(m + 1) * 256][hh * 64:(hh + 1) * 64, :],
                                        start=True, stop=True)
                    P.op(pe, mms, reads=(r_qb, r_kb), writes=(rb,), tag="mms")
            P.fence(pe)

        def emit_softmax(m):
            for hh in range(2):
                bank, rb = scores_banks(m)[hh]
                for i in range(2):
                    def d_mx(v, hh=hh, i=i, bank=bank, m=m):
                        return v.reduce_max(col(f"sb{hh}{i}_{m % 2}"),
                                            bank[:, i * 256:(i + 1) * 256],
                                            axis=AX, negate=True)
                    P.op(dv, d_mx, reads=(rb,), writes=(r_cols,), tag="smx")
                    P.fence(dv)
                    ebap, rebm = eb_of(m)
                    wr = (rebm, r_cols) if m % 2 == 0 else (rebm, r_lnb, r_cols)
                    def a_e(a, hh=hh, i=i, bank=bank, m=m, ebap=ebap):
                        return a.activation(ebap[:, (hh * 2 + i) * 256:(hh * 2 + i + 1) * 256],
                                            bank[:, i * 256:(i + 1) * 256], AF.Exp,
                                            bias=col(f"sb{hh}{i}_{m % 2}"),
                                            accum_out=col(f"se{hh}{i}_{m % 2}"))
                    P.op(ac, a_e, reads=(rb, r_cols), writes=wr, tag="exp")
                    P.fence(ac)

        def emit_ET(m):
            ebap, rebm = eb_of(m)
            xbar_T(ebap[:, 0:1024].rearrange("p (k n) -> p k n", k=8),
                   HQ[:, 3072:4096].rearrange("p (k n) -> p k n", k=8),
                   rebm, r_etb, "et")

        def emit_admd(m):
            for hh in range(2):
                bank, rb = scores_banks(m)[hh]
                h = 2 * m + hh
                src = ETB if hh == 0 else ETB1
                for i in range(2):
                    for jb in range(2):
                        def mmad(t, i=i, jb=jb, h=h, bank=bank, src=src):
                            return t.matmul(bank[:, i * 64:(i + 1) * 64],
                                            src[:, (i * 2 + jb) * 128:(i * 2 + jb + 1) * 128],
                                            VP[:, jb * 512 + h * 64:jb * 512 + h * 64 + 64],
                                            start=(jb == 0), stop=(jb == 1))
                        P.op(pe, mmad, reads=(r_etb, r_vp), writes=(rb,), tag="mmad")
                for i in range(2):
                    def mmam(t, i=i, m=m, hh=hh, bank=bank):
                        return t.matmul(bank[:, 128 + i * 65:128 + i * 65 + 65],
                                        SQF[:, m * 256 + i * 128:m * 256 + (i + 1) * 128]
                                        [hh * 64:(hh + 1) * 64, :],
                                        MEMZ[hh * 64:(hh + 1) * 64, m * 65:m * 65 + 65],
                                        start=True, stop=True)
                    P.op(pe, mmam, reads=(r_sqf, r_memz), writes=(rb,), tag="mmam")
            P.fence(pe)

        def emit_combine(m):
            for hh in range(2):
                bank, rb = scores_banks(m)[hh]
                h = 2 * m + hh
                for i in range(2):
                    def d_cb(v, i=i, hh=hh, h=h, bank=bank, m=m):
                        v.reciprocal(col(f"rs{hh}{i}"), col(f"se{hh}{i}_{m % 2}"))
                        v.reciprocal(col(f"rd{hh}{i}"),
                                     bank[:, 128 + i * 65 + 64:128 + i * 65 + 65])
                        v.tensor_scalar(SC2[:, 0:64], bank[:, i * 64:(i + 1) * 64],
                                        col(f"rs{hh}{i}"), None, op0=ALU.mult)
                        return v.scalar_tensor_tensor(
                            ATT[:, i * 512 + h * 64:i * 512 + (h + 1) * 64],
                            bank[:, 128 + i * 65:128 + i * 65 + 64], col(f"rd{hh}{i}"),
                            SC2[:, 0:64], op0=ALU.mult, op1=ALU.add)
                    P.op(dv, d_cb, reads=(rb, r_cols), writes=(r_att, r_sc2, r_cols),
                         tag="cb")
                    P.fence(dv)

        emit_scores(0)
        emit_softmax(0)
        for m in range(4):
            if m < 3:
                emit_scores(m + 1)
                emit_softmax(m + 1)
            emit_ET(m)
            emit_admd(m)
            emit_combine(m)
        for m in range(4):
            for parh in range(2):
                def d_mz(v, m=m, parh=parh):
                    return v.tensor_mul(
                        MEMZ[parh * 64:(parh + 1) * 64, m * 65:m * 65 + 64],
                        pM[parh * 64:(parh + 1) * 64,
                           m * 128 + parh * 64:m * 128 + parh * 64 + 64],
                        GB[parh * 64:(parh + 1) * 64, m * 64:(m + 1) * 64])
                P.op(dv, d_mz, reads=(r_pM, r_gb), writes=(r_memz,), tag="mz")
        P.fence(dv)
        if c == 0:
            dbg("att", ATT, [r_att])
            dbg("memz1", MEMZ[:, :], [r_memz])
        for i in range(2):
            xbar_T(ATT[:, i * 512:(i + 1) * 512].rearrange("p (k n) -> p k n", k=4),
                   ATF.rearrange("p (k n) -> p k n", k=4)[:, :, i * 128:(i + 1) * 128],
                   r_att, r_atf, "at")
        for tau in range(2):
            for half in range(2):
                for K in range(4):
                    def mmo(t, tau=tau, half=half, K=K):
                        return t.matmul(pD[:, half * 512:(half + 1) * 512],
                                        ATF[:, K * 256 + tau * 128:K * 256 + (tau + 1) * 128],
                                        wob[:, K * 1024 + half * 512:K * 1024 + (half + 1) * 512],
                                        start=(K == 0), stop=(K == 3))
                    P.op(pe, mmo, reads=(r_atf, r_awt), writes=(r_pD,), tag="mmo")
                P.fence(pe)
                def d_x2(v, tau=tau, half=half):
                    return v.tensor_add(X2[:, tau * 1024 + half * 512:tau * 1024 + (half + 1) * 512],
                                        X1[:, tau * 1024 + half * 512:tau * 1024 + (half + 1) * 512],
                                        pD[:, half * 512:(half + 1) * 512])
                P.op(dv, d_x2, reads=(r_pD, r_x1), writes=(r_x2,), tag="dx2")
                P.fence(dv)
        if c == 0:
            dbg("x2c0", X2[:, :], [r_x2])

    for tau in range(2):
        m0, r0 = ln_stats(r_xin, XIN[:, tau * 1024:(tau + 1) * 1024], f"f1{tau}_0")

    reg_ffn_cols(0)
    x_quant("f1", 0, 0, 0, XIN, r_xin, X2[:, 0:1024], r_x2)
    x_quant_T(0)
    x_quant("f1", 0, 0, 1, XIN, r_xin, X2[:, 1024:2048], r_x2)
    x_quant_T(1)

    for c in range(NCH):
        par = c % 2
        tb_f1 = (lambda c=c: attn_ln(c, 0)) if stage >= 2 else None
        ffn(c, XIN, r_xin, X1, r_x1, X2, r_x2, "f1", xpath_done=True, tail_b=tb_f1)
        if c + 1 < NCH:
            x_load(c + 1)
        if stage == 1:
            src, r_src = X1, r_x1
        else:
            attn_ln(c, 1)
            attn(c, ln_done=True)
            if c + 1 < NCH:
                for tau in range(2):
                    ln_stats(r_xin, XIN[:, tau * 1024:(tau + 1) * 1024],
                             f"f1{tau}_{1 - par}")
            if stage == 2:
                src, r_src = X2, r_x2
            else:
                for tau in range(2):
                    ln_stats(r_x2, X2[:, tau * 1024:(tau + 1) * 1024], f"f2{tau}_{par}")
                reg_ffn_cols(1)
                x_quant("f2", 1, par, 0, X2, r_x2, X1[:, 0:1024], r_x1)
                x_quant_T(0)
                x_quant("f2", 1, par, 1, X2, r_x2, X1[:, 1024:2048], r_x1)
                x_quant_T(1)
                if c + 1 < NCH:
                    def ta_f2(c=c, par=par):
                        x_quant("f1", 0, 1 - par, 0, XIN, r_xin, X1[:, 0:1024], r_x1)
                        x_quant_T(0)
                        x_quant("f1", 0, 1 - par, 1, XIN, r_xin, X1[:, 0:1024], r_x1)
                    def tb_f2(c=c):
                        x_quant_T(1)
                else:
                    ta_f2 = tb_f2 = None
                ffn(c, X2, r_x2, X1, r_x1, X1, r_x1, "f2", xpath_done=True,
                    tail_a=ta_f2, tail_b=tb_f2)
                src, r_src = X1, r_x1
        def st(s, c=c, src=src):
            return s.dma_start(
                out=y_h[c * 256:(c + 1) * 256, :].rearrange("(t p) n -> p t n", p=128),
                in_=src.ap().rearrange("p (t n) -> p t n", t=2))
        P.op(d_o, st, reads=(r_src,), writes=(), tag=f"store{c}")
        P.fence(d_o)
        if stage == 1 and c + 1 < NCH:
            for tau in range(2):
                ln_stats(r_xin, XIN[:, tau * 1024:(tau + 1) * 1024],
                         f"f1{tau}_{1 - par}")
            x_quant("f1", 0, 1 - par, 0, XIN, r_xin, X2[:, 0:1024], r_x2)
            x_quant_T(0)
            x_quant("f1", 0, 1 - par, 1, XIN, r_xin, X2[:, 1024:2048], r_x2)
            x_quant_T(1)

    P.ops["sync"].append({"fn": lambda s: None, "waits": [(d_o, 16 * NCH)],
                          "inc": None, "tag": "final"})

    with nc.Block() as block:
        sems = []
        for e in engs:
            g = nc.semaphore(f"sem_{e.name}")
            e.sem = g.__enter__()
            sems.append(g)
        block.tensor(lambda t: P.run_body(t, "tensor"))
        block.vector(lambda v: P.run_body(v, "vector"))
        block.scalar(lambda a: P.run_body(a, "scalar"))
        block.gpsimd(lambda g: P.run_body(g, "gpsimd"))
        block.sync(lambda s: P.run_body(s, "sync"))
    for g in reversed(ctx):
        g.__exit__(None, None, None)
    for g in reversed(sems):
        g.__exit__(None, None, None)
    return nc


_NC_CACHE = {}


def kernel(L0_attn_x, L0_attn_Wq, L0_attn_Wk, L0_attn_Wv, L0_attn_betas, L0_attn_Wo,
           L0_ffn_w_gate, L0_ffn_w_up, L0_ffn_w_down):
    if "nc" not in _NC_CACHE:
        _NC_CACHE["nc"] = build(STAGE)
    nc = _NC_CACHE["nc"]
    x = np.ascontiguousarray(np.asarray(L0_attn_x, dtype=np.float32))
    common = {
        "wq": np.ascontiguousarray(np.asarray(L0_attn_Wq, dtype=np.float32)),
        "wk": np.ascontiguousarray(np.asarray(L0_attn_Wk, dtype=np.float32)),
        "wv": np.ascontiguousarray(np.asarray(L0_attn_Wv, dtype=np.float32)),
        "wo": np.ascontiguousarray(np.asarray(L0_attn_Wo, dtype=np.float32)),
        "betas": np.ascontiguousarray(
            np.asarray(L0_attn_betas, dtype=np.float32).reshape(1, 512)),
        "wg": np.ascontiguousarray(np.asarray(L0_ffn_w_gate, dtype=np.float32)),
        "wu": np.ascontiguousarray(np.asarray(L0_ffn_w_up, dtype=np.float32)),
        "wd": np.ascontiguousarray(np.asarray(L0_ffn_w_down, dtype=np.float32)),
    }
    in_maps = [dict(common, x=np.ascontiguousarray(x[b])) for b in range(8)]
    res = run_bass_kernel_spmd(nc, in_maps, core_ids=list(range(8)))
    return np.stack([r["y"] for r in res.results], axis=0)


# version 18 (final)
# speedup vs baseline: 1.0946x; 1.0132x over previous
import numpy as np
import concourse.bass as bass
import concourse.mybir as mybir
from concourse.bass_utils import run_bass_kernel_spmd

F32 = mybir.dt.float32
BF16 = mybir.dt.bfloat16
FP8 = mybir.dt.float8e4
AF = mybir.ActivationFunctionType
ALU = mybir.AluOpType
AX = mybir.AxisListType.X

C23 = 12582912.0
C52 = 6755399441055744.0
EPS = 1e-5
NCH, D, INNER = 16, 1024, 4096
SEQ = NCH * 256
STAGE = 3
DEBUG_TAPS = ()
LAST_COL_IDS = {}


class Res:
    __slots__ = ("name", "writer", "readers")

    def __init__(self, name):
        self.name = name
        self.writer = None
        self.readers = []


class EngS:
    def __init__(self, name, body, step=1):
        self.name, self.body, self.step = name, body, step
        self.n = 0
        self.sem = None
        self.pending = []
        self.last_op = None
        self.max_waited = {}


class Prog:
    def __init__(self):
        self.ops = {k: [] for k in ("tensor", "vector", "scalar", "gpsimd", "sync")}

    def fence(self, e):
        if e.last_op is None:
            assert not e.pending
            return
        e.n += e.step
        e.last_op["inc"] = (e, e.step)
        for kind, r in e.pending:
            if kind == "w":
                r.writer = (e, e.n)
                r.readers = []
            else:
                r.readers.append((e, e.n))
        e.pending = []
        e.last_op = None

    def op(self, e, fn, reads=(), writes=(), tag=""):
        waits = {}

        def need(src):
            if src is None:
                return
            s, v = src
            if s is e or e.max_waited.get(s, -1) >= v:
                return
            waits[s] = max(waits.get(s, 0), v)

        for r in reads:
            need(r.writer)
        for r in writes:
            need(r.writer)
            for rd in r.readers:
                need(rd)
        for s, v in waits.items():
            if v > s.n:
                raise RuntimeError(f"wait on unfenced {s.name}={v} (have {s.n}) at {tag}")
            e.max_waited[s] = v
        rec = {"fn": fn, "waits": list(waits.items()), "inc": None, "tag": tag}
        self.ops[e.body].append(rec)
        e.last_op = rec
        for r in reads:
            e.pending.append(("r", r))
        for r in writes:
            e.pending.append(("w", r))
        return rec

    def handoff(self, dsts, srcs):
        ev = []
        for r in srcs:
            if r.writer is not None:
                ev.append(r.writer)
            ev.extend(r.readers)
        for d in dsts:
            d.writer = None
            d.readers = list(ev)

    def run_body(self, engobj, body):
        for rec in self.ops[body]:
            for s, v in rec["waits"]:
                engobj.wait_ge(s.sem, v)
            instr = rec["fn"](engobj)
            if rec["inc"] is not None:
                s, k = rec["inc"]
                instr.then_inc(s.sem, k)


def build(stage=STAGE):
    nc = bass.Bass("TRN2", target_bir_lowering=False, debug=False)
    P = Prog()
    pe = EngS("pe", "tensor")
    dv = EngS("dv", "vector")
    ac = EngS("ac", "scalar")
    gp = EngS("gp", "gpsimd")
    d_x = EngS("d_x", "sync", 16)
    d_w = EngS("d_w", "gpsimd", 16)
    d_o = EngS("d_o", "sync", 16)
    d_t = EngS("d_t", "sync", 16)
    engs = [pe, dv, ac, gp, d_x, d_w, d_o, d_t]

    x_h = nc.dram_tensor("x", [SEQ, D], F32, kind="ExternalInput").ap()
    wq_h = nc.dram_tensor("wq", [D, 512], F32, kind="ExternalInput").ap()
    wk_h = nc.dram_tensor("wk", [D, 512], F32, kind="ExternalInput").ap()
    wv_h = nc.dram_tensor("wv", [D, 512], F32, kind="ExternalInput").ap()
    wo_h = nc.dram_tensor("wo", [512, D], F32, kind="ExternalInput").ap()
    bet_h = nc.dram_tensor("betas", [1, 512], F32, kind="ExternalInput").ap()
    wg_h = nc.dram_tensor("wg", [INNER, D], F32, kind="ExternalInput").ap()
    wu_h = nc.dram_tensor("wu", [INNER, D], F32, kind="ExternalInput").ap()
    wd_h = nc.dram_tensor("wd", [D, INNER], F32, kind="ExternalInput").ap()
    y_h = nc.dram_tensor("y", [SEQ, D], F32, kind="ExternalOutput").ap()

    ctx = []

    def sb(name, shape, dt):
        g = nc.sbuf_tensor(name, shape, dt)
        t = g.__enter__()
        ctx.append(g)
        return t

    def pst(name, shape, dt):
        g = nc.psum_tensor(name, shape, dt)
        t = g.__enter__()
        ctx.append(g)
        return t

    wgT = sb("wgT", [128, 8 * INNER], FP8)
    wuT = sb("wuT", [128, 8 * INNER], FP8)
    wdT = sb("wdT", [128, 32 * 1024], FP8)
    wqb = sb("wqb", [128, 8 * 512], BF16)
    wkb = sb("wkb", [128, 8 * 512], BF16)
    wvb = sb("wvb", [128, 8 * 512], BF16)
    wob = sb("wob", [128, 4 * 1024], BF16)
    XIN = sb("XIN", [128, 2048], F32)
    X1 = sb("X1", [128, 2048], F32)
    X2 = sb("X2", [128, 2048], F32)
    HT = sb("HT", [128, 4096], F32)
    HQ = sb("HQ", [128, 4096], BF16)
    FMP = sb("FMP", [128, 4096], BF16)
    XQ = sb("XQ", [128, 1024], BF16)
    XQF = sb("XQF", [128, 1024], BF16)
    SCR = sb("SCR", [128, 2048], F32)
    SC2 = sb("SC2", [128, 1024], F32)
    HTb = HT.ap().bitcast(BF16)
    FMPf = FMP.ap().bitcast(F32)

    QB = HTb[:, 0:1024]
    KB = HTb[:, 1024:2048]
    SQF = HTb[:, 2048:3072]
    SKF = HTb[:, 3072:4096]
    SKT = HTb[:, 4096:5120]
    VT = HTb[:, 5120:6144]
    VP = HTb[:, 6144:7168]
    ATT = HTb[:, 7168:8192]
    LNB = HQ.ap()[:, 0:2048]
    EB0 = HQ.ap()[:, 2048:3072]
    EB1 = HQ.ap()[:, 0:1024]
    ETB = HQ.ap()[:, 3072:3584]
    ETB1 = HQ.ap()[:, 3584:4096]
    ATF = FMP.ap()[:, 0:1024]
    LNF = FMP.ap()[:, 1024:3072]
    MEMZ = sb("MEMZ", [128, 260], BF16)
    ZS = sb("ZS", [128, 4], F32)
    GB = sb("GB", [128, 256], F32)
    OGF = sb("OGF", [128, 512], BF16)
    COLS = sb("COLS", [128, 176], F32)
    IDN = sb("IDN", [128, 128], BF16)
    ONE1 = sb("ONE1", [1, 128], BF16)
    ONE1F = sb("ONE1F", [1, 128], F32)
    ONEC = sb("ONEC", [128, 1], F32)
    ROWS = SCR.ap()[0:1, 0:1024]
    ROWB = HTb[0:1, 6144:7168]
    STAT = sb("STAT", [128, 16], F32)
    WST = HT.ap()[:, 0:4096]
    WQ3 = HQ.ap()[:, 0:2048]

    pT0 = pst("pT0", [128, 1024], BF16)
    pT1 = pst("pT1", [128, 1024], BF16)
    pW0 = pst("pW0", [128, 512], F32)
    pW1 = pst("pW1", [128, 512], F32)
    pW2 = pst("pW2", [128, 512], F32)
    pD = pst("pD", [128, 1024], F32)
    pM = pst("pM", [128, 512], F32)

    R = Res
    r_wgT, r_wuT, r_wdT = R("wgT"), R("wuT"), R("wdT")
    r_awt = R("attn_w")
    r_xin, r_x1, r_x2 = R("XIN"), R("X1"), R("X2")
    r_ht, r_xq, r_xqf = R("HT"), R("XQ"), R("XQF")
    r_hqq = [R(f"HQ{i}") for i in range(4)]
    r_fmpq = [R(f"FMP{i}") for i in range(4)]
    r_scr, r_sc2 = R("SCR"), R("SC2")
    r_u, r_t1 = R("U"), R("T1")
    r_s2b0, r_s2b1, r_s2b2 = R("s2b0"), R("s2b1"), R("s2b2")
    r_wstA, r_wstB = R("wstA"), R("wstB")
    r_lnb, r_lnf = R("LNB"), R("LNF")
    r_qb, r_kb, r_sqf, r_skf, r_skt = R("QB"), R("KB"), R("SQF"), R("SKF"), R("SKT")
    r_vt, r_vp, r_eb, r_etb = R("VT"), R("VP"), R("EB"), R("ETB")
    r_eb1 = R("EB1")
    r_att, r_atf = R("ATT"), R("ATF")
    r_memz, r_zs, r_gb, r_ogf = R("MEMZ"), R("ZS"), R("GB"), R("OGF")
    r_cols, r_idn, r_ones = R("COLS"), R("IDN"), R("ONES")
    r_rows, r_stat = R("ROWS"), R("STAT")
    r_wst, r_wq3 = R("WST"), R("WQ3")
    r_tb1, r_tb3 = R("TQ1"), R("TQ3")
    r_tcol = R("TCOL")
    r_pT0, r_pT1 = R("pT0"), R("pT1")
    r_pW0, r_pW1, r_pW2, r_pD, r_pM = R("pW0"), R("pW1"), R("pW2"), R("pD"), R("pM")

    r_qb2, r_kb2 = r_qb, r_kb
    FFN_SIDE = [r_ht, r_u, r_t1, r_s2b0, r_s2b1, r_s2b2] + r_hqq + r_fmpq
    ATTN_SIDE = [r_qb, r_kb, r_sqf, r_skf, r_skt, r_vt, r_vp, r_att,
                 r_lnb, r_eb, r_eb1, r_etb, r_atf, r_lnf, r_sc2]

    col_ids = {}
    global LAST_COL_IDS
    LAST_COL_IDS = col_ids

    def col(name):
        if name not in col_ids:
            col_ids[name] = len(col_ids)
            assert len(col_ids) <= 176, "out of COLS"
        i = col_ids[name]
        return COLS[:, i:i + 1]

    def dbg(name, ap, reads):
        if name not in DEBUG_TAPS:
            return
        h = nc.dram_tensor("dbg_" + name, list(ap.shape), ap.dtype,
                           kind="ExternalOutput").ap()
        P.op(d_o, lambda s, h=h, ap=ap: s.dma_start(out=h, in_=ap),
             reads=tuple(reads), tag="dbg_" + name)
        P.fence(d_o)

    def g_setup(g):
        g.memset(IDN[:, :], 0.0)
        g.affine_select(out=IDN[:, :], in_=IDN[:, :], compare_op=ALU.not_equal,
                        fill=1.0, base=0, pattern=[[-1, 128]], channel_multiplier=1)
        g.memset(ONE1[:, :], 1.0)
        g.memset(ONE1F[:, :], 1.0)
        g.memset(ONEC[:, :], 1.0)
        g.memset(ZS[:, :], 1.0 / 64.0)
        return g.memset(MEMZ[:, :], 0.0)
    P.op(gp, g_setup, writes=(r_idn, r_ones, r_zs, r_memz), tag="setup")
    P.fence(gp)

    def d_flag(v):
        v.memset(col("k75"), 0.75)
        v.drain()
        v.tensor_scalar(col("flag"), col("k75"), C52, None, op0=ALU.add)
        v.drain()
        v.tensor_scalar(col("flag"), col("flag"), C52, None, op0=ALU.subtract)
        v.drain()
        return v.tensor_scalar(col("om"), col("flag"), -1.0, 1.0, op0=ALU.mult, op1=ALU.add)
    P.op(dv, d_flag, writes=(r_cols,), tag="flag")
    P.fence(dv)
    dbg("cols_flag", COLS[:, :], [r_cols])

    for (h, t, kk) in ((wq_h, wqb, 8), (wk_h, wkb, 8), (wv_h, wvb, 8), (wo_h, wob, 4)):
        def ld(g, h=h, t=t, kk=kk):
            return g.dma_start(out=t.ap().rearrange("p (k n) -> p k n", k=kk),
                               in_=h.rearrange("(k p) n -> p k n", p=128))
        P.op(d_w, ld, writes=(r_awt,), tag="wload")
        P.fence(d_w)
    P.op(d_w, lambda g: g.dma_start(out=ROWS[0:1, 0:512], in_=bet_h),
         writes=(r_rows, r_sc2), tag="betas")
    P.fence(d_w)

    def x_load(c):
        def ld(s, c=c):
            return s.dma_start(
                out=XIN.ap().rearrange("p (t n) -> p t n", t=2),
                in_=x_h[c * 256:(c + 1) * 256, :].rearrange("(t p) n -> p t n", p=128))
        P.op(d_x, ld, writes=(r_xin,), tag=f"xload{c}")
        P.fence(d_x)

    x_load(0)

    P.op(ac, lambda a: a.activation(ROWS[0:1, 512:1024], ROWS[0:1, 0:512], AF.Sigmoid),
         reads=(r_rows,), writes=(r_rows,), tag="sig")
    P.fence(ac)

    def d_ogrow(v):
        v.tensor_scalar(ROWS[0:1, 0:512], ROWS[0:1, 512:1024], -1.0, 1.0,
                        op0=ALU.mult, op1=ALU.add)
        v.tensor_copy(ROWB[0:1, 0:512], ROWS[0:1, 0:512])
        return v.tensor_copy(ROWB[0:1, 512:1024], ROWS[0:1, 512:1024])
    P.op(dv, d_ogrow, reads=(r_rows, r_sc2), writes=(r_rows, r_ht), tag="ogrow")
    P.fence(dv)
    dbg("rowb", ROWB[0:1, 0:1024], [r_rows])
    dbg("rows", ROWS[0:1, 0:1024], [r_rows])

    P.op(pe, lambda t: t.matmul(pW0[:, 0:512], ONE1[0:1, :], ROWB[0:1, 0:512],
                                start=True, stop=True),
         reads=(r_rows, r_ones), writes=(r_pW0,), tag="ogbc")
    P.fence(pe)
    P.op(dv, lambda v: v.tensor_copy(OGF[:, :], pW0[:, 0:512]),
         reads=(r_pW0,), writes=(r_ogf,), tag="ogf")
    P.fence(dv)
    dbg("ogf", OGF[:, :], [r_ogf])
    dbg("gbf", GB[:, :], [r_gb])
    for m in range(4):
        for par in range(2):
            def gbc(t, m=m, par=par):
                return t.matmul(pW1[par * 64:(par + 1) * 64, m * 64:(m + 1) * 64],
                                ONE1[0:1, 0:64],
                                ROWB[0:1, 512 + (2 * m + par) * 64:512 + (2 * m + par + 1) * 64],
                                start=True, stop=True,
                                tile_position=(0, par * 64))
            P.op(pe, gbc, reads=(r_rows, r_ones), writes=(r_pW1,), tag="gbc")
    P.fence(pe)
    P.op(dv, lambda v: v.tensor_copy(GB[:, :], pW1[:, 0:256]),
         reads=(r_pW1,), writes=(r_gb,), tag="gbf")
    P.fence(dv)

    def tern_A_tile(wname, src_fn, i):
        acc, tmp = col("acc_" + wname), col("tmp_" + wname)
        off = (i % 2) * 1024
        def ld(sq, i=i, off=off):
            return sq.dma_start(out=WST[:, off:off + 1024], in_=src_fn(i))
        P.op(d_x, ld, writes=(r_wstA,), tag="tldA")
        P.fence(d_x)
        def a_abs(a, off=off):
            return a.activation(SCR[:, 1024:2048], WST[:, off:off + 1024],
                                AF.Abs, accum_out=tmp)
        P.op(ac, a_abs, reads=(r_wstA,), writes=(r_t1, r_tcol), tag="abs")
        P.fence(ac)
        if i == 0:
            P.op(dv, lambda v: v.tensor_copy(acc, tmp),
                 reads=(r_tcol,), writes=(r_tcol,), tag="acc")
        else:
            def d_acc(v):
                v.drain()
                return v.tensor_add(acc, acc, tmp)
            P.op(dv, d_acc, reads=(r_tcol,), writes=(r_tcol,), tag="acc")
        P.fence(dv)

    def tern_scale(wname):
        acc = col("acc_" + wname)
        P.op(pe, lambda t: t.matmul(pW2[0:1, 0:1], acc, ONEC[:, :],
                                    start=True, stop=True),
             reads=(r_tcol, r_ones), writes=(r_pW2,), tag="tsum")
        P.fence(pe)
        def d_scale(v):
            v.tensor_scalar(STAT[0:1, 14:15], pW2[0:1, 0:1],
                            1.0 / (INNER * D), 1e-5, op0=ALU.mult, op1=ALU.max)
            v.drain()
            return v.reciprocal(STAT[0:1, 15:16], STAT[0:1, 14:15])
        P.op(dv, d_scale, reads=(r_pW2,), writes=(r_stat,), tag="scale")
        P.fence(dv)
        P.op(pe, lambda t: t.matmul(pW2[:, 4:6], ONE1F[0:1, :],
                                    STAT[0:1, 14:16], start=True, stop=True),
             reads=(r_stat, r_ones), writes=(r_pW2,), tag="scbc")
        P.fence(pe)
        def d_cols(v, wname=wname):
            v.tensor_copy(col("swrec_" + wname), pW2[:, 4:5])
            v.tensor_copy(col("wsc_" + wname), pW2[:, 5:6])
            v.drain()
            return v.tensor_mul(col("wsck_" + wname), col("wsc_" + wname),
                                col("flag"))
        P.op(dv, d_cols, reads=(r_pW2, r_cols), writes=(r_cols,), tag="wcols")
        P.fence(dv)

    def tern_B_parity(i):
        if i % 2 == 0:
            q1, q3 = SC2[:, 0:1024], SCR[:, 0:1024]
            rq1, rq3 = (r_s2b0, r_s2b1, r_sc2), (r_u,)
        else:
            q1, q3 = X1[:, 0:1024], X1[:, 1024:2048]
            rq1, rq3 = (r_tb1,), (r_tb3,)
        pb, rpb = (pT0, r_pT0) if i % 2 == 0 else (pT1, r_pT1)
        return q1, q3, rq1, rq3, pb, rpb

    def tern_B_front(wname, src_fn, i):
        wsc, wsck = col("wsc_" + wname), col("wsck_" + wname)
        off = (2 + i % 2) * 1024
        q1, q3, rq1, rq3, pb, rpb = tern_B_parity(i)
        def ld(g, i=i, off=off):
            return g.dma_start(out=WST[:, off:off + 1024], in_=src_fn(i))
        P.op(d_w, ld, writes=(r_wstB,), tag="tldB")
        P.fence(d_w)
        def a_t1(a, off=off, wsc=wsc, q1=q1):
            return a.activation(q1, WST[:, off:off + 1024],
                                AF.Copy, bias=C23, scale=wsc)
        P.op(ac, a_t1, reads=(r_wstB, r_cols), writes=rq1, tag="t1")
        P.fence(ac)
        def g_t3(g, off=off, wsck=wsck, q3=q3):
            return g.tensor_scalar(q3, WST[:, off:off + 1024],
                                   wsck, C52, op0=ALU.mult, op1=ALU.add)
        P.op(gp, g_t3, reads=(r_wstB, r_cols), writes=rq3, tag="t3")
        P.fence(gp)
        def d_q(v, i=i, q1=q1, q3=q3):
            v.tensor_scalar(q1, q1, C23, col("om"),
                            op0=ALU.subtract, op1=ALU.mult)
            v.scalar_tensor_tensor(q1, q3, C52, q1,
                                   op0=ALU.subtract, op1=ALU.add)
            wslot = (i % 2) * 1024
            return v.tensor_scalar(WQ3[:, wslot:wslot + 1024],
                                   q1, 1.0, -1.0, op0=ALU.min, op1=ALU.max)
        P.op(dv, d_q, reads=rq1 + rq3 + (r_cols,),
             writes=rq1 + rq3 + (r_wq3,), tag="q3")
        P.fence(dv)

    def tern_B_back(r_dst, dst_fn, i):
        q1, q3, rq1, rq3, pb, rpb = tern_B_parity(i)
        wslot = (i % 2) * 1024
        for b in range(8):
            def tr(t, b=b, wslot=wslot, pb=pb):
                return t.transpose(pb[:, b * 128:(b + 1) * 128],
                                   WQ3[:, wslot + b * 128:wslot + (b + 1) * 128],
                                   IDN[:, :])
            P.op(pe, tr, reads=(r_wq3, r_idn), writes=(rpb,), tag="wtr")
        P.fence(pe)
        def a_ev(a, i=i, pb=pb):
            return a.copy(dst_fn(i), pb[:, :].rearrange("p (k n) -> p k n", k=8))
        P.op(ac, a_ev, reads=(rpb,), writes=(r_dst,), tag="wev")
        P.fence(ac)

    TERN = [
        ("g", lambda i: wg_h[i * 128:(i + 1) * 128, :], r_wgT,
         lambda i: wgT.ap().rearrange("p (k n) -> p k n", k=8)[:, :, i * 128:(i + 1) * 128]),
        ("u", lambda i: wu_h[i * 128:(i + 1) * 128, :], r_wuT,
         lambda i: wuT.ap().rearrange("p (k n) -> p k n", k=8)[:, :, i * 128:(i + 1) * 128]),
        ("d", lambda i: wd_h[(i // 4) * 128:(i // 4 + 1) * 128,
                             (i % 4) * 1024:(i % 4 + 1) * 1024], r_wdT,
         lambda i: wdT.ap().rearrange("p (k n) -> p k n", k=32)
         [:, (i % 4) * 8:(i % 4) * 8 + 8, (i // 4) * 128:(i // 4 + 1) * 128]),
    ]
    for i in range(32):
        tern_A_tile(TERN[0][0], TERN[0][1], i)
    tern_scale(TERN[0][0])
    for k in range(3):
        wname, src_fn, r_dst, dst_fn = TERN[k]
        for i in range(32):
            tern_B_front(wname, src_fn, i)
            if k + 1 < 3:
                tern_A_tile(TERN[k + 1][0], TERN[k + 1][1], i)
            if i > 0:
                tern_B_back(r_dst, dst_fn, i - 1)
        tern_B_back(r_dst, dst_fn, 31)
        if k + 1 < 3:
            tern_scale(TERN[k + 1][0])
    P.handoff([r_x1], [r_tb1, r_tb3])

    dbg("wgt", wgT[:, 0:4096], [r_wgT])
    dbg("wdt", wdT[:, 0:4096], [r_wdT])

    def ln_stats(src_res, src_ap, pref):
        mcol, rcol = col("m_" + pref), col("r_" + pref)
        def d_st(v, src_ap=src_ap):
            v.bn_stats(STAT[:, 0:6], src_ap[:, 0:512])
            v.bn_stats(STAT[:, 6:12], src_ap[:, 512:1024])
            v.drain()
            v.bn_aggr(STAT[:, 12:14], STAT[:, 0:12])
            v.drain()
            v.tensor_copy(mcol, STAT[:, 12:13])
            return v.tensor_scalar(col("tv"), STAT[:, 13:14], EPS, None, op0=ALU.add)
        P.op(dv, d_st, reads=(src_res,), writes=(r_stat, r_cols), tag="lnst_" + pref)
        P.fence(dv)
        def a_r(a):
            a.activation(col("tv2"), col("tv"), AF.Ln)
            a.drain()
            return a.activation(rcol, col("tv2"), AF.Exp, scale=-0.5)
        P.op(ac, a_r, reads=(r_cols,), writes=(r_cols,), tag="lnr_" + pref)
        P.fence(ac)
        return mcol, rcol

    _trot = [0]

    def transpose_group(src_ap_fn, n, pbank_ignored, r_ignored, dst_ap, r_src, r_dst,
                        tag):
        rot = _trot[0]
        _trot[0] += 1
        pbank, r_pbank = (pT0, r_pT0) if rot % 2 == 0 else (pT1, r_pT1)
        for b in range(n):
            def tr(t, b=b, pbank=pbank):
                return t.transpose(pbank[:, b * 128:(b + 1) * 128], src_ap_fn(b), IDN[:, :])
            P.op(pe, tr, reads=(r_src, r_idn), writes=(r_pbank,), tag="tr_" + tag)
        P.fence(pe)
        eng = dv if (rot // 2) % 2 == 0 else ac
        def ev(e, pbank=pbank):
            return e.tensor_copy(dst_ap,
                                 pbank[:, 0:n * 128].rearrange("p (k n) -> p k n", k=n))
        if eng is dv:
            P.op(dv, ev, reads=(r_pbank,), writes=(r_dst,), tag="ev_" + tag)
            P.fence(dv)
        else:
            P.op(ac, lambda a, pbank=pbank: a.copy(
                dst_ap, pbank[:, 0:n * 128].rearrange("p (k n) -> p k n", k=n)),
                reads=(r_pbank,), writes=(r_dst,), tag="ev_" + tag)
            P.fence(ac)

    def xbar_T(in_ap, out_ap, r_src, r_dst, tag):
        def tr(sq):
            return sq.dma_start(out=out_ap, in_=in_ap, transpose=True)
        P.op(d_t, tr, reads=(r_src,), writes=(r_dst,), tag="xT_" + tag)
        P.fence(d_t)

    U_AP = SCR[:, 0:1024]
    T1_AP = SCR[:, 1024:2048]
    SILU_AP = (SC2[:, 0:512], SC2[:, 512:1024])
    SILU_RES = (r_s2b0, r_s2b1)

    def x_quant(pref, ap_, par, tau, in_buf, r_in, t3ap, r_t3):
        mcol = col(f"m_{pref}{tau}_{par}")
        rcol = col(f"r_{pref}{tau}_{par}")
        def d_u(v, tau=tau, mcol=mcol, ap_=ap_):
            v.tensor_scalar(U_AP, in_buf[:, tau * 1024:(tau + 1) * 1024], mcol,
                            None, op0=ALU.subtract)
            return v.reduce_max(col(f"mxm{tau}_{ap_}"), U_AP, axis=AX,
                                apply_absolute_value=True)
        P.op(dv, d_u, reads=(r_in, r_cols), writes=(r_u, r_cols), tag="u")
        def d_sc(v, tau=tau, rcol=rcol, ap_=ap_):
            q = f"{tau}_{ap_}"
            v.drain()
            v.tensor_scalar(col(f"mcl{q}"), col(f"mxm{q}"), 1e-5, None, op0=ALU.max)
            v.tensor_mul(col(f"rmx{q}"), rcol, col(f"mxm{q}"))
            v.drain()
            v.reciprocal(col(f"srec{q}"), col(f"mcl{q}"))
            v.tensor_scalar(col(f"rmx{q}"), col(f"rmx{q}"), 1e-5, None, op0=ALU.max)
            v.drain()
            v.tensor_scalar(col(f"s{q}"), col(f"srec{q}"), 127.0, None, op0=ALU.mult)
            v.tensor_mul(col(f"cg{q}"), col(f"rmx{q}"), col("swrec_g"))
            v.tensor_mul(col(f"cu{q}"), col(f"rmx{q}"), col("swrec_u"))
            v.drain()
            v.tensor_mul(col(f"skm{q}"), col(f"s{q}"), col("flag"))
            v.tensor_scalar(col(f"cg{q}"), col(f"cg{q}"), 1.0 / 127.0, None,
                            op0=ALU.mult)
            return v.tensor_scalar(col(f"cu{q}"), col(f"cu{q}"), 1.0 / 127.0,
                                   None, op0=ALU.mult)
        P.op(dv, d_sc, reads=(r_cols,), writes=(r_cols,), tag="sc")
        P.fence(dv)
        def a_t1(a, tau=tau, ap_=ap_):
            return a.activation(T1_AP, U_AP, AF.Copy, bias=C23,
                                scale=col(f"s{tau}_{ap_}"))
        P.op(ac, a_t1, reads=(r_u, r_cols), writes=(r_t1,), tag="xt1")
        P.fence(ac)
        def g_t3(g, tau=tau, ap_=ap_, t3ap=t3ap):
            return g.tensor_scalar(t3ap, U_AP, col(f"skm{tau}_{ap_}"), C52,
                                   op0=ALU.mult, op1=ALU.add)
        P.op(gp, g_t3, reads=(r_u, r_cols), writes=(r_t3,), tag="xt3")
        P.fence(gp)
        def d_x(v, t3ap=t3ap):
            v.tensor_scalar(T1_AP, T1_AP, C23, col("om"), op0=ALU.subtract,
                            op1=ALU.mult)
            return v.scalar_tensor_tensor(XQ[:, :], t3ap, C52, T1_AP,
                                          op0=ALU.subtract, op1=ALU.add)
        P.op(dv, d_x, reads=(r_t1, r_t3, r_cols), writes=(r_t1, r_xq), tag="xq")
        P.fence(dv)

    def x_quant_T(tau):
        if tau == 0:
            dst = XQF.ap().rearrange("p (k n) -> p k n", k=8)
            rdst = r_xqf
        else:
            dst = FMP[:, 0:1024].rearrange("p (k n) -> p k n", k=8)
            rdst = r_fmpq[0]
        xbar_T(XQ.ap().rearrange("p (k n) -> p k n", k=8), dst, r_xq, rdst, "xq")

    FFN_COL_NAMES = ("mxm", "mcl", "srec", "s", "skm", "rmx", "cg", "cu",
                     "mxh", "t2c", "s2r", "s2", "c2", "sig", "sigk")

    def reg_ffn_cols(ap_):
        for t_ in range(2):
            for n_ in FFN_COL_NAMES:
                col(f"{n_}{t_}_{ap_}")
            for sl_ in range(8):
                col(f"hmx{sl_}_{t_}_{ap_}")

    def ffn(c, in_buf, r_in, out_buf, r_out, scr_buf, r_scrb, pref,
            xpath_done=False, tail_a=None, tail_b=None, tail_gu=None):
        ap_ = 0 if pref == "f1" else 1
        reg_ffn_cols(ap_)
        dsts = FFN_SIDE if not xpath_done else \
            [r for r in FFN_SIDE if r not in tuple(r_fmpq) + (r_t1, r_u)]
        P.handoff(dsts, ATTN_SIDE + [r_wst, r_wstA, r_wstB, r_wq3, r_rows, r_scr])
        par = c % 2

        def emit_xpath(tau):
            x_quant(pref, ap_, par, tau, in_buf, r_in,
                    scr_buf[:, tau * 1024:(tau + 1) * 1024], r_scrb)

        def emit_xqT(tau):
            x_quant_T(tau)

        def lhsT_ap(tau, K):
            src = XQF if tau == 0 else FMP
            return src[:, K * 128:(K + 1) * 128]

        GU_BANKS = [(pW0, r_pW0), (pW1, r_pW1), (pW2, r_pW2)]

        def emit_gu_slice(tau, sl, nb):
            banks = GU_BANKS
            rsrc = r_xqf if tau == 0 else r_fmpq[0]
            if True:
                bg, rbg = banks[nb % 3]; nb += 1
                bu, rbu = banks[nb % 3]; nb += 1
                for K in range(8):
                    def mmg(t, K=K, sl=sl, tau=tau, bg=bg):
                        return t.matmul(bg[:, :], lhsT_ap(tau, K),
                                        wgT[:, K * INNER + sl * 512:K * INNER + (sl + 1) * 512],
                                        start=(K == 0), stop=(K == 7))
                    P.op(pe, mmg, reads=(rsrc, r_wgT), writes=(rbg,), tag="mmg")
                P.fence(pe)
                for K in range(8):
                    def mmu(t, K=K, sl=sl, tau=tau, bu=bu):
                        return t.matmul(bu[:, :], lhsT_ap(tau, K),
                                        wuT[:, K * INNER + sl * 512:K * INNER + (sl + 1) * 512],
                                        start=(K == 0), stop=(K == 7))
                    P.op(pe, mmu, reads=(rsrc, r_wuT), writes=(rbu,), tag="mmu")
                P.fence(pe)
                sap, sres = SILU_AP[sl % 2], SILU_RES[sl % 2]
                def a_silu(a, tau=tau, bg=bg, sap=sap, ap_=ap_):
                    return a.activation(sap, bg[:, :], AF.Silu,
                                        scale=col(f"cg{tau}_{ap_}"))
                P.op(ac, a_silu, reads=(rbg, r_cols), writes=(sres,), tag="silu")
                P.fence(ac)
                def d_h(v, sl=sl, tau=tau, bu=bu, sap=sap, ap_=ap_):
                    v.tensor_mul(HT[:, sl * 512:(sl + 1) * 512], sap, bu[:, :])
                    return v.reduce_max(col(f"hmx{sl}_{tau}_{ap_}"),
                                        HT[:, sl * 512:(sl + 1) * 512], axis=AX,
                                        apply_absolute_value=True)
                P.op(dv, d_h, reads=(sres, rbu), writes=(r_ht, r_cols), tag="h")
                P.fence(dv)
            return nb

        def emit_s2chain(tau):
            q = f"{tau}_{ap_}"
            c0 = col_ids[f"hmx0_{q}"]
            assert col_ids[f"hmx7_{q}"] == c0 + 7
            def d_s2(v, q=q, c0=c0):
                v.drain()
                v.reduce_max(col(f"mxh{q}"), COLS[:, c0:c0 + 8], axis=AX)
                v.drain()
                v.tensor_mul(col(f"t2c{q}"), col(f"cu{q}"), col(f"mxh{q}"))
                v.drain()
                v.tensor_scalar(col(f"t2c{q}"), col(f"t2c{q}"), 1e-5, None,
                                op0=ALU.max)
                v.drain()
                v.reciprocal(col(f"s2r{q}"), col(f"t2c{q}"))
                v.tensor_mul(col(f"c2{q}"), col(f"t2c{q}"), col("swrec_d"))
                v.drain()
                v.tensor_scalar(col(f"s2{q}"), col(f"s2r{q}"), 127.0, None,
                                op0=ALU.mult)
                v.tensor_scalar(col(f"c2{q}"), col(f"c2{q}"), 0.5 / 127.0, None,
                                op0=ALU.mult)
                v.drain()
                v.tensor_mul(col(f"sig{q}"), col(f"s2{q}"), col(f"cu{q}"))
                v.drain()
                return v.tensor_mul(col(f"sigk{q}"), col(f"sig{q}"), col("flag"))
            P.op(dv, d_s2, reads=(r_cols,), writes=(r_cols,), tag="s2")
            P.fence(dv)

        def emit_hqq_T(tau, q):
            xbar_T(HQ[:, q * 1024:(q + 1) * 1024].rearrange("p (k n) -> p k n", k=8),
                   FMP[:, q * 1024:(q + 1) * 1024].rearrange("p (k n) -> p k n", k=8),
                   r_hqq[q], r_fmpq[q], f"hq{q}")

        def emit_round_quarter(tau, q, defer_dma=False):
            t3reg = scr_buf[:, 1024:2048]
            if True:
                hsl = HT[:, q * 1024:(q + 1) * 1024]
                def a_h1(a, tau=tau, hsl=hsl, ap_=ap_):
                    return a.activation(T1_AP, hsl, AF.Copy, bias=C23,
                                        scale=col(f"sig{tau}_{ap_}"))
                P.op(ac, a_h1, reads=(r_ht, r_cols), writes=(r_t1,), tag="h1")
                P.fence(ac)
                def g_h3(g, tau=tau, hsl=hsl, t3reg=t3reg, ap_=ap_):
                    return g.tensor_scalar(t3reg, hsl, col(f"sigk{tau}_{ap_}"), C52,
                                           op0=ALU.mult, op1=ALU.add)
                P.op(gp, g_h3, reads=(r_ht, r_cols), writes=(r_scrb,), tag="h3")
                P.fence(gp)
                def d_hq(v, q=q, t3reg=t3reg):
                    v.tensor_scalar(T1_AP, T1_AP, C23, col("om"), op0=ALU.subtract,
                                    op1=ALU.mult)
                    return v.scalar_tensor_tensor(HQ[:, q * 1024:(q + 1) * 1024],
                                                  t3reg, C52, T1_AP,
                                                  op0=ALU.subtract, op1=ALU.add)
                P.op(dv, d_hq, reads=(r_t1, r_scrb, r_cols),
                     writes=(r_t1, r_hqq[q]), tag="hq")
                P.fence(dv)
            if not defer_dma:
                emit_hqq_T(tau, q)

        DOWN_KORDER = list(range(8, 32)) + list(range(8))

        def emit_down(tau):
            for idx, K in enumerate(DOWN_KORDER):
                for half in range(2):
                    def mmd(t, K=K, half=half, idx=idx):
                        return t.matmul(pD[:, half * 512:(half + 1) * 512],
                                        FMP[:, K * 128:(K + 1) * 128],
                                        wdT[:, K * 1024 + half * 512:K * 1024 + (half + 1) * 512],
                                        start=(idx == 0), stop=(idx == 31))
                    P.op(pe, mmd, reads=(r_fmpq[K // 8], r_wdT), writes=(r_pD,), tag="mmd")
            P.fence(pe)

        def emit_downevac(tau):
            to = tau * 1024
            for half in range(2):
                def d_out(v, to=to, half=half, tau=tau, ap_=ap_):
                    return v.scalar_tensor_tensor(
                        out_buf[:, to + half * 512:to + (half + 1) * 512],
                        pD[:, half * 512:(half + 1) * 512], col(f"c2{tau}_{ap_}"),
                        in_buf[:, to + half * 512:to + (half + 1) * 512],
                        op0=ALU.mult, op1=ALU.add)
                P.op(dv, d_out, reads=(r_pD, r_in, r_cols), writes=(r_out,), tag="dout")
                P.fence(dv)

        if not xpath_done:
            emit_xpath(0)
            emit_xqT(0)
            emit_xpath(1)
        nb = 0
        for sl in range(8):
            nb = emit_gu_slice(0, sl, nb)
        if not xpath_done:
            emit_xqT(1)
        emit_s2chain(0)
        nb = 0
        for sl in range(8):
            if sl % 2 == 0:
                emit_round_quarter(0, sl // 2, defer_dma=(sl == 0))
            nb = emit_gu_slice(1, sl, nb)
        if tail_gu is not None:
            tail_gu()
        emit_hqq_T(0, 0)
        emit_s2chain(1)
        emit_down(0)
        for q in range(4):
            emit_round_quarter(1, q)
        if tail_a is not None:
            tail_a()
        emit_downevac(0)
        emit_down(1)
        if tail_b is not None:
            tail_b()
        emit_downevac(1)

    def attn_ln(c, tau):
        mcol, rcol = ln_stats(r_x1, X1[:, tau * 1024:(tau + 1) * 1024], f"at{tau}")
        def d_ln(v, tau=tau, mcol=mcol, rcol=rcol):
            return v.tensor_scalar(LNB[:, tau * 1024:(tau + 1) * 1024],
                                   X1[:, tau * 1024:(tau + 1) * 1024],
                                   mcol, rcol, op0=ALU.subtract, op1=ALU.mult)
        P.op(dv, d_ln, reads=(r_x1, r_cols), writes=(r_lnb, r_hqq[tau]), tag="lnA")
        P.fence(dv)

    def attn(c, ln_done=False, tail_x2=None):
        dsts = ATTN_SIDE if not ln_done else [r for r in ATTN_SIDE if r is not r_lnb]
        P.handoff(dsts, FFN_SIDE + [r_rows, r_xq, r_wq3])
        par = c % 2
        def d_zm(v):
            for m_ in range(4):
                v.tensor_copy(MEMZ[:, m_ * 65 + 64:m_ * 65 + 65], ZS[:, m_:m_ + 1])
            return v.nop()
        P.op(dv, d_zm, reads=(r_zs,), writes=(r_memz,), tag="zmem")
        P.fence(dv)
        for tau in range(2):
            if not ln_done:
                attn_ln(c, tau)
            xbar_T(LNB[:, tau * 1024:(tau + 1) * 1024].rearrange("p (k n) -> p k n", k=8),
                   LNF.rearrange("p (k n) -> p k n", k=8)[:, :, tau * 128:(tau + 1) * 128],
                   r_lnb, r_lnf, f"lnf{tau}")
        for m in range(4):
            bank, rb = (pW0, r_pW0) if m < 2 else (pW1, r_pW1)
            boff = (m % 2) * 256
            for K in range(8):
                def mmq(t, m=m, K=K, bank=bank, boff=boff):
                    return t.matmul(bank[:, boff:boff + 256],
                                    wqb[:, K * 512 + m * 128:K * 512 + (m + 1) * 128],
                                    LNF[:, K * 256:(K + 1) * 256],
                                    start=(K == 0), stop=(K == 7))
                P.op(pe, mmq, reads=(r_awt, r_lnf), writes=(rb,), tag="mmq")
        P.fence(pe)
        for m in range(4):
            for K in range(8):
                def mmk(t, m=m, K=K):
                    return t.matmul(pD[:, m * 256:(m + 1) * 256],
                                    wkb[:, K * 512 + m * 128:K * 512 + (m + 1) * 128],
                                    LNF[:, K * 256:(K + 1) * 256],
                                    start=(K == 0), stop=(K == 7))
                P.op(pe, mmk, reads=(r_awt, r_lnf), writes=(r_pD,), tag="mmk")
        P.fence(pe)
        for m in range(4):
            bank = pW0 if m < 2 else pW1
            rb = r_pW0 if m < 2 else r_pW1
            boff = (m % 2) * 256
            def a_eq(a, m=m, bank=bank, boff=boff):
                return a.activation(SC2[:, m * 256:(m + 1) * 256], bank[:, boff:boff + 256],
                                    AF.Exp)
            P.op(ac, a_eq, reads=(rb,), writes=(r_sc2,), tag="eq")
            P.fence(ac)
            def a_qc(a, m=m, bank=bank, boff=boff):
                return a.activation(QB[:, m * 256:(m + 1) * 256], bank[:, boff:boff + 256],
                                    AF.Copy, scale=0.125)
            P.op(ac, a_qc, reads=(rb,), writes=(r_qb,), tag="qc")
            P.fence(ac)
            def d_q(v, m=m, bank=bank, boff=boff):
                v.tensor_scalar(SC2[:, m * 256:(m + 1) * 256], SC2[:, m * 256:(m + 1) * 256],
                                1.0, None, op0=ALU.min)
                return v.scalar_tensor_tensor(SQF[:, m * 256:(m + 1) * 256],
                                              bank[:, boff:boff + 256], 0.0,
                                              SC2[:, m * 256:(m + 1) * 256],
                                              op0=ALU.max, op1=ALU.add)
            P.op(dv, d_q, reads=(rb, r_sc2), writes=(r_sqf, r_sc2), tag="dq")
            P.fence(dv)
        if c == 0:
            dbg("lnb", LNB, [r_lnb])
            dbg("qb", QB, [r_qb])
            dbg("sqf", SQF, [r_sqf])
        for m in range(4):
            def a_ek(a, m=m):
                return a.activation(SC2[:, m * 256:(m + 1) * 256], pD[:, m * 256:(m + 1) * 256],
                                    AF.Exp)
            P.op(ac, a_ek, reads=(r_pD,), writes=(r_sc2,), tag="ek")
            P.fence(ac)
            def a_kc(a, m=m):
                return a.activation(KB[:, m * 256:(m + 1) * 256],
                                    pD[:, m * 256:(m + 1) * 256], AF.Copy)
            P.op(ac, a_kc, reads=(r_pD,), writes=(r_kb,), tag="kc")
            P.fence(ac)
            def d_k(v, m=m):
                v.tensor_scalar(SC2[:, m * 256:(m + 1) * 256], SC2[:, m * 256:(m + 1) * 256],
                                1.0, None, op0=ALU.min)
                v.scalar_tensor_tensor(SKF[:, m * 256:(m + 1) * 256],
                                       pD[:, m * 256:(m + 1) * 256], 0.0,
                                       SC2[:, m * 256:(m + 1) * 256],
                                       op0=ALU.max, op1=ALU.add,
                                       accum_out=col(f"zd{m}"))
                v.drain()
                return v.tensor_add(ZS[:, m:m + 1], ZS[:, m:m + 1], col(f"zd{m}"))
            P.op(dv, d_k, reads=(r_pD, r_sc2), writes=(r_skf, r_sc2, r_zs, r_cols),
                 tag="dk")
            P.fence(dv)
        if c == 0:
            dbg("kb", KB, [r_kb])
            dbg("skf", SKF, [r_skf])
            dbg("zs", ZS[:, :], [r_zs])
        for tau in range(2):
            bank, rb = (pW0, r_pW0) if tau == 0 else (pW1, r_pW1)
            for K in range(8):
                def mmv(t, tau=tau, K=K, bank=bank):
                    return t.matmul(bank[:, :],
                                    LNF[:, K * 256 + tau * 128:K * 256 + (tau + 1) * 128],
                                    wvb[:, K * 512:(K + 1) * 512],
                                    start=(K == 0), stop=(K == 7))
                P.op(pe, mmv, reads=(r_awt, r_lnf), writes=(rb,), tag="mmv")
            P.fence(pe)
            def a_vc(a, tau=tau, bank=bank):
                return a.activation(VT[:, tau * 512:(tau + 1) * 512], bank[:, :], AF.Copy)
            P.op(ac, a_vc, reads=(rb,), writes=(r_vt,), tag="vc")
            P.fence(ac)
            def d_v(v, tau=tau, bank=bank):
                return v.tensor_mul(VP[:, tau * 512:(tau + 1) * 512], bank[:, :], OGF[:, :])
            P.op(dv, d_v, reads=(rb, r_ogf), writes=(r_vp,), tag="dv")
            P.fence(dv)
        if c == 0:
            dbg("vt", VT, [r_vt])
            dbg("vp", VP, [r_vp])
        for m in range(4):
            pb, rpb = (pT0, r_pT0) if m < 2 else (pT1, r_pT1)
            for b in range(2):
                def tr(t, m=m, b=b, pb=pb):
                    return t.transpose(pb[:, (2 * (m % 2) + b) * 128:(2 * (m % 2) + b + 1) * 128],
                                       SKF[:, m * 256 + b * 128:m * 256 + (b + 1) * 128],
                                       IDN[:, :])
                P.op(pe, tr, reads=(r_skf, r_idn), writes=(rpb,), tag="sktr")
            if m % 2 == 1:
                P.fence(pe)
                def ev(v, m=m, pb=pb):
                    return v.tensor_copy(
                        SKT.rearrange("p (t q n) -> p t q n", t=2, q=4)
                        [:, :, m - 1:m + 1, :],
                        pb[:, 0:512].rearrange("p (q t n) -> p t q n", q=2, t=2))
                P.op(dv, ev, reads=(rpb,), writes=(r_skt,), tag="skev")
                P.fence(dv)
        for m in range(4):
            for tau in range(2):
                def mmm(t, m=m, tau=tau, c=c):
                    return t.matmul(pM[:, m * 128:(m + 1) * 128],
                                    SKT[:, tau * 512 + m * 128:tau * 512 + (m + 1) * 128],
                                    VT[:, tau * 512 + m * 128:tau * 512 + (m + 1) * 128],
                                    start=(c == 0 and m == 0 and tau == 0), stop=False)
                P.op(pe, mmm, reads=(r_skt, r_vt), writes=(r_pM,), tag="mmm")
        P.fence(pe)
        for hh_ in range(2):
            for i_ in range(2):
                for n_ in ("rd", "rs"):
                    col(f"{n_}{hh_}{i_}")
                for p_ in range(2):
                    for n_ in ("sb", "se"):
                        col(f"{n_}{hh_}{i_}_{p_}")

        def eb_of(m):
            return (EB0, r_eb) if m % 2 == 0 else (EB1, r_eb1)

        def scores_banks(m):
            if m % 2 == 0:
                return ((pW0, r_pW0), (pW1, r_pW1))
            return ((pD[:, 0:512], r_pD), (pD[:, 512:1024], r_pD))

        def emit_scores(m):
            for hh in range(2):
                bank, rb = scores_banks(m)[hh]
                for i in range(2):
                    def mms(t, m=m, hh=hh, i=i, bank=bank):
                        return t.matmul(bank[:, i * 256:(i + 1) * 256],
                                        QB[:, m * 256 + i * 128:m * 256 + (i + 1) * 128]
                                        [hh * 64:(hh + 1) * 64, :],
                                        KB[:, m * 256:(m + 1) * 256][hh * 64:(hh + 1) * 64, :],
                                        start=True, stop=True)
                    P.op(pe, mms, reads=(r_qb, r_kb), writes=(rb,), tag="mms")
            P.fence(pe)

        def emit_softmax(m):
            for hh in range(2):
                bank, rb = scores_banks(m)[hh]
                for i in range(2):
                    def d_mx(v, hh=hh, i=i, bank=bank, m=m):
                        return v.reduce_max(col(f"sb{hh}{i}_{m % 2}"),
                                            bank[:, i * 256:(i + 1) * 256],
                                            axis=AX, negate=True)
                    P.op(dv, d_mx, reads=(rb,), writes=(r_cols,), tag="smx")
                    P.fence(dv)
                    ebap, rebm = eb_of(m)
                    wr = (rebm, r_cols) if m % 2 == 0 else (rebm, r_lnb, r_cols)
                    def a_e(a, hh=hh, i=i, bank=bank, m=m, ebap=ebap):
                        return a.activation(ebap[:, (hh * 2 + i) * 256:(hh * 2 + i + 1) * 256],
                                            bank[:, i * 256:(i + 1) * 256], AF.Exp,
                                            bias=col(f"sb{hh}{i}_{m % 2}"),
                                            accum_out=col(f"se{hh}{i}_{m % 2}"))
                    P.op(ac, a_e, reads=(rb, r_cols), writes=wr, tag="exp")
                    P.fence(ac)

        def emit_ET(m):
            ebap, rebm = eb_of(m)
            xbar_T(ebap[:, 0:1024].rearrange("p (k n) -> p k n", k=8),
                   HQ[:, 3072:4096].rearrange("p (k n) -> p k n", k=8),
                   rebm, r_etb, "et")

        def emit_admd(m):
            for hh in range(2):
                bank, rb = scores_banks(m)[hh]
                h = 2 * m + hh
                src = ETB if hh == 0 else ETB1
                for i in range(2):
                    for jb in range(2):
                        def mmad(t, i=i, jb=jb, h=h, bank=bank, src=src):
                            return t.matmul(bank[:, i * 64:(i + 1) * 64],
                                            src[:, (i * 2 + jb) * 128:(i * 2 + jb + 1) * 128],
                                            VP[:, jb * 512 + h * 64:jb * 512 + h * 64 + 64],
                                            start=(jb == 0), stop=(jb == 1))
                        P.op(pe, mmad, reads=(r_etb, r_vp), writes=(rb,), tag="mmad")
                for i in range(2):
                    def mmam(t, i=i, m=m, hh=hh, bank=bank):
                        return t.matmul(bank[:, 128 + i * 65:128 + i * 65 + 65],
                                        SQF[:, m * 256 + i * 128:m * 256 + (i + 1) * 128]
                                        [hh * 64:(hh + 1) * 64, :],
                                        MEMZ[hh * 64:(hh + 1) * 64, m * 65:m * 65 + 65],
                                        start=True, stop=True)
                    P.op(pe, mmam, reads=(r_sqf, r_memz), writes=(rb,), tag="mmam")
            P.fence(pe)

        def emit_combine(m):
            for hh in range(2):
                bank, rb = scores_banks(m)[hh]
                h = 2 * m + hh
                for i in range(2):
                    def d_cb(v, i=i, hh=hh, h=h, bank=bank, m=m):
                        v.reciprocal(col(f"rs{hh}{i}"), col(f"se{hh}{i}_{m % 2}"))
                        v.reciprocal(col(f"rd{hh}{i}"),
                                     bank[:, 128 + i * 65 + 64:128 + i * 65 + 65])
                        v.tensor_scalar(SC2[:, 0:64], bank[:, i * 64:(i + 1) * 64],
                                        col(f"rs{hh}{i}"), None, op0=ALU.mult)
                        return v.scalar_tensor_tensor(
                            ATT[:, i * 512 + h * 64:i * 512 + (h + 1) * 64],
                            bank[:, 128 + i * 65:128 + i * 65 + 64], col(f"rd{hh}{i}"),
                            SC2[:, 0:64], op0=ALU.mult, op1=ALU.add)
                    P.op(dv, d_cb, reads=(rb, r_cols), writes=(r_att, r_sc2, r_cols),
                         tag="cb")
                    P.fence(dv)

        emit_scores(0)
        emit_softmax(0)
        for m in range(4):
            if m < 3:
                emit_scores(m + 1)
                emit_softmax(m + 1)
            emit_ET(m)
            emit_admd(m)
            emit_combine(m)
        for m in range(4):
            for parh in range(2):
                def d_mz(v, m=m, parh=parh):
                    return v.tensor_mul(
                        MEMZ[parh * 64:(parh + 1) * 64, m * 65:m * 65 + 64],
                        pM[parh * 64:(parh + 1) * 64,
                           m * 128 + parh * 64:m * 128 + parh * 64 + 64],
                        GB[parh * 64:(parh + 1) * 64, m * 64:(m + 1) * 64])
                P.op(dv, d_mz, reads=(r_pM, r_gb), writes=(r_memz,), tag="mz")
        P.fence(dv)
        if c == 0:
            dbg("att", ATT, [r_att])
            dbg("memz1", MEMZ[:, :], [r_memz])
        for i in range(2):
            xbar_T(ATT[:, i * 512:(i + 1) * 512].rearrange("p (k n) -> p k n", k=4),
                   ATF.rearrange("p (k n) -> p k n", k=4)[:, :, i * 128:(i + 1) * 128],
                   r_att, r_atf, "at")
        for tau in range(2):
            for half in range(2):
                for K in range(4):
                    def mmo(t, tau=tau, half=half, K=K):
                        return t.matmul(pD[:, half * 512:(half + 1) * 512],
                                        ATF[:, K * 256 + tau * 128:K * 256 + (tau + 1) * 128],
                                        wob[:, K * 1024 + half * 512:K * 1024 + (half + 1) * 512],
                                        start=(K == 0), stop=(K == 3))
                    P.op(pe, mmo, reads=(r_atf, r_awt), writes=(r_pD,), tag="mmo")
                P.fence(pe)
                def d_x2(v, tau=tau, half=half):
                    return v.tensor_add(X2[:, tau * 1024 + half * 512:tau * 1024 + (half + 1) * 512],
                                        X1[:, tau * 1024 + half * 512:tau * 1024 + (half + 1) * 512],
                                        pD[:, half * 512:(half + 1) * 512])
                P.op(dv, d_x2, reads=(r_pD, r_x1), writes=(r_x2,), tag="dx2")
                P.fence(dv)
            if tau == 0 and tail_x2 is not None:
                tail_x2()
        if c == 0:
            dbg("x2c0", X2[:, :], [r_x2])

    for tau in range(2):
        m0, r0 = ln_stats(r_xin, XIN[:, tau * 1024:(tau + 1) * 1024], f"f1{tau}_0")

    reg_ffn_cols(0)
    x_quant("f1", 0, 0, 0, XIN, r_xin, X2[:, 0:1024], r_x2)
    x_quant_T(0)
    x_quant("f1", 0, 0, 1, XIN, r_xin, X2[:, 1024:2048], r_x2)
    x_quant_T(1)

    for c in range(NCH):
        par = c % 2
        tb_f1 = (lambda c=c: attn_ln(c, 0)) if stage >= 2 else None
        ffn(c, XIN, r_xin, X1, r_x1, X2, r_x2, "f1", xpath_done=True, tail_b=tb_f1)
        if c + 1 < NCH:
            x_load(c + 1)
        if stage == 1:
            src, r_src = X1, r_x1
        else:
            attn_ln(c, 1)
            if stage >= 3:
                reg_ffn_cols(1)
                def tx2(par=par):
                    ln_stats(r_x2, X2[:, 0:1024], f"f20_{par}")
                    x_quant("f2", 1, par, 0, X2, r_x2, FMPf[:, 512:1536], r_lnf)
                    x_quant_T(0)
            else:
                tx2 = None
            attn(c, ln_done=True, tail_x2=tx2)
            if c + 1 < NCH:
                for tau in range(2):
                    ln_stats(r_xin, XIN[:, tau * 1024:(tau + 1) * 1024],
                             f"f1{tau}_{1 - par}")
            if stage == 2:
                src, r_src = X2, r_x2
            else:
                ln_stats(r_x2, X2[:, 1024:2048], f"f21_{par}")
                x_quant("f2", 1, par, 1, X2, r_x2, X1[:, 1024:2048], r_x1)
                x_quant_T(1)
                if c + 1 < NCH:
                    def ta_f2(c=c, par=par):
                        x_quant("f1", 0, 1 - par, 0, XIN, r_xin, X1[:, 0:1024], r_x1)
                        x_quant_T(0)
                        x_quant("f1", 0, 1 - par, 1, XIN, r_xin, X1[:, 0:1024], r_x1)
                    def tb_f2(c=c):
                        x_quant_T(1)
                else:
                    ta_f2 = tb_f2 = None
                ffn(c, X2, r_x2, X1, r_x1, X1, r_x1, "f2", xpath_done=True,
                    tail_a=ta_f2, tail_b=tb_f2)
                src, r_src = X1, r_x1
        def st(s, c=c, src=src):
            return s.dma_start(
                out=y_h[c * 256:(c + 1) * 256, :].rearrange("(t p) n -> p t n", p=128),
                in_=src.ap().rearrange("p (t n) -> p t n", t=2))
        P.op(d_o, st, reads=(r_src,), writes=(), tag=f"store{c}")
        P.fence(d_o)
        if stage == 1 and c + 1 < NCH:
            for tau in range(2):
                ln_stats(r_xin, XIN[:, tau * 1024:(tau + 1) * 1024],
                         f"f1{tau}_{1 - par}")
            x_quant("f1", 0, 1 - par, 0, XIN, r_xin, X2[:, 0:1024], r_x2)
            x_quant_T(0)
            x_quant("f1", 0, 1 - par, 1, XIN, r_xin, X2[:, 1024:2048], r_x2)
            x_quant_T(1)

    P.ops["sync"].append({"fn": lambda s: None, "waits": [(d_o, 16 * NCH)],
                          "inc": None, "tag": "final"})

    with nc.Block() as block:
        sems = []
        for e in engs:
            g = nc.semaphore(f"sem_{e.name}")
            e.sem = g.__enter__()
            sems.append(g)
        block.tensor(lambda t: P.run_body(t, "tensor"))
        block.vector(lambda v: P.run_body(v, "vector"))
        block.scalar(lambda a: P.run_body(a, "scalar"))
        block.gpsimd(lambda g: P.run_body(g, "gpsimd"))
        block.sync(lambda s: P.run_body(s, "sync"))
    for g in reversed(ctx):
        g.__exit__(None, None, None)
    for g in reversed(sems):
        g.__exit__(None, None, None)
    return nc


_NC_CACHE = {}


def kernel(L0_attn_x, L0_attn_Wq, L0_attn_Wk, L0_attn_Wv, L0_attn_betas, L0_attn_Wo,
           L0_ffn_w_gate, L0_ffn_w_up, L0_ffn_w_down):
    if "nc" not in _NC_CACHE:
        _NC_CACHE["nc"] = build(STAGE)
    nc = _NC_CACHE["nc"]
    x = np.ascontiguousarray(np.asarray(L0_attn_x, dtype=np.float32))
    common = {
        "wq": np.ascontiguousarray(np.asarray(L0_attn_Wq, dtype=np.float32)),
        "wk": np.ascontiguousarray(np.asarray(L0_attn_Wk, dtype=np.float32)),
        "wv": np.ascontiguousarray(np.asarray(L0_attn_Wv, dtype=np.float32)),
        "wo": np.ascontiguousarray(np.asarray(L0_attn_Wo, dtype=np.float32)),
        "betas": np.ascontiguousarray(
            np.asarray(L0_attn_betas, dtype=np.float32).reshape(1, 512)),
        "wg": np.ascontiguousarray(np.asarray(L0_ffn_w_gate, dtype=np.float32)),
        "wu": np.ascontiguousarray(np.asarray(L0_ffn_w_up, dtype=np.float32)),
        "wd": np.ascontiguousarray(np.asarray(L0_ffn_w_down, dtype=np.float32)),
    }
    in_maps = [dict(common, x=np.ascontiguousarray(x[b])) for b in range(8)]
    res = run_bass_kernel_spmd(nc, in_maps, core_ids=list(range(8)))
    return np.stack([r["y"] for r in res.results], axis=0)
